# version 54
# baseline (speedup 1.0000x reference)
import sys
sys.path.insert(0, "/opt/trn_rl_repo")
import math
from contextlib import ExitStack

import numpy as np
import concourse.bass as bass
import concourse.mybir as mybir
from concourse.bass_utils import run_bass_kernel_spmd

F32 = mybir.dt.float32
BF16 = mybir.dt.bfloat16
AF = mybir.ActivationFunctionType
ALU = mybir.AluOpType
AXX = mybir.AxisListType.X

ENGINES = ("pe", "act", "dve", "pool", "sp")
NDQ = 6
NCC = 2


class _Op:
    __slots__ = ("idx", "engine", "fn", "reads", "writes", "kind", "tag",
                 "chan", "inc_amt", "needs_inc", "count_after", "waits",
                 "clock", "_deps", "pre_wait")

    def __init__(self, idx, engine, fn, reads, writes, kind, tag):
        self.idx = idx
        self.engine = engine
        self.fn = fn
        self.reads = tuple(reads)
        self.writes = tuple(writes)
        self.kind = kind
        self.tag = tag
        self.chan = engine
        self.inc_amt = 1
        self.needs_inc = kind != "compute"
        self.count_after = None
        self.waits = None
        self.clock = None
        self.pre_wait = None


class Prog:

    def __init__(self, nc):
        self.nc = nc
        self.ops = []
        self.last_writer = {}
        self.readers = {}
        self.dq_counts = {}
        self.cc_count = 0
        self.chan_last = {}
        self.stats = {"waits": 0, "incs": 0}

    def emit(self, engine, fn, reads=(), writes=(), dma=False, cc=False,
             tag="", after=()):
        assert engine in ENGINES, engine
        kind = "cc" if cc else ("dma" if dma else "compute")
        op = _Op(len(self.ops), engine, fn, reads, writes, kind, tag)
        if kind == "dma":
            j = self.dq_counts.get(engine, 0)
            self.dq_counts[engine] = j + 1
            op.chan = f"dq_{engine}_{j % NDQ}"
            op.inc_amt = 16
            op.count_after = 16 * (j // NDQ + 1)
            if j >= NDQ:
                op.pre_wait = (op.chan, 16 * (j // NDQ))
        elif kind == "cc":
            j = self.cc_count
            self.cc_count = j + 1
            op.chan = f"cc_{j % NCC}"
            op.inc_amt = 1
            op.count_after = j // NCC + 1
            if j >= NCC:
                op.pre_wait = (op.chan, j // NCC)
        self.ops.append(op)
        deps = list(after)
        for r in op.reads:
            w = self.last_writer.get(r)
            if w is not None:
                deps.append(w)
        for r in op.writes:
            w = self.last_writer.get(r)
            if w is not None:
                deps.append(w)
            for rd in self.readers.get(r, ()):
                deps.append(rd)
        op._deps = [d for d in dict.fromkeys(deps)
                    if not (d.engine == "pe" and op.engine == "pe"
                            and d.kind == "compute" and op.kind == "compute")
                    and d is not op]
        for d in op._deps:
            if d.kind == "compute":
                d.needs_inc = True
        for r in op.reads:
            self.readers.setdefault(r, []).append(op)
        for r in op.writes:
            self.last_writer[r] = op
            self.readers[r] = []
        self.chan_last[op.chan if kind != "compute" else engine] = op
        return op

    def pe(self, fn, reads=(), writes=(), tag=""):
        return self.emit("pe", fn, reads, writes, tag=tag)

    def act(self, fn, reads=(), writes=(), tag=""):
        return self.emit("act", fn, reads, writes, tag=tag)

    def dve(self, fn, reads=(), writes=(), tag=""):
        return self.emit("dve", fn, reads, writes, tag=tag)

    def pool(self, fn, reads=(), writes=(), tag=""):
        return self.emit("pool", fn, reads, writes, tag=tag)

    def dma(self, engine, out, in_, reads=(), writes=(), tag="", **kw):
        return self.emit(engine,
                         lambda e, o=out, i=in_, kw=kw: e.dma_start(out=o, in_=i, **kw),
                         reads, writes, dma=True, tag=tag)

    def barrier(self):
        pending = list(dict.fromkeys(self.chan_last.values()))
        for e in ENGINES:
            self.emit(e, lambda eng: eng.nop(), tag="barrier", after=pending)

    def build(self, extra_guard=True):
        nc = self.nc
        counts = {}
        for op in self.ops:
            if op.kind == "compute":
                if op.needs_inc:
                    counts[op.chan] = counts.get(op.chan, 0) + 1
                    op.count_after = counts[op.chan]
            else:
                counts[op.chan] = max(counts.get(op.chan, 0), op.count_after)
        clocks = {e: {} for e in ENGINES}
        for op in self.ops:
            clk = clocks[op.engine]
            if op.pre_wait is not None:
                c, v = op.pre_wait
                if clk.get(c, 0) >= v:
                    op.pre_wait = None
                else:
                    clk[c] = v
            waits = {}
            for d in op._deps:
                if not d.needs_inc:
                    continue
                tgt = d.count_after
                if clk.get(d.chan, 0) >= tgt:
                    continue
                waits[d.chan] = max(waits.get(d.chan, 0), tgt)
            for chan, tgt in waits.items():
                clk[chan] = max(clk.get(chan, 0), tgt)
            for d in op._deps:
                if d.needs_inc and d.clock is not None and clk.get(d.chan, 0) >= d.count_after:
                    for c2, v2 in d.clock.items():
                        if v2 > clk.get(c2, 0):
                            clk[c2] = v2
            op.waits = waits
            snap = dict(clk)
            snap[op.chan] = max(snap.get(op.chan, 0), op.count_after or 0)
            op.clock = snap
            self.stats["waits"] += len(waits) + (1 if op.pre_wait else 0)
            self.stats["incs"] += 1 if op.needs_inc else 0

        final_counts = dict(counts)
        sems = {}
        stack = ExitStack()
        for c in sorted(final_counts):
            if final_counts[c] > 0:
                sems[c] = stack.enter_context(nc.semaphore("sem_" + c))
        block = stack.enter_context(nc.Block())
        by_engine = {e: [op for op in self.ops if op.engine == e] for e in ENGINES}
        eng_attr = {"pe": "tensor", "act": "scalar", "dve": "vector",
                    "pool": "gpsimd", "sp": "sync"}

        def emit_engine(e, ops):
            def body(eng):
                for op in ops:
                    if op.pre_wait is not None:
                        eng.wait_ge(sems[op.pre_wait[0]], op.pre_wait[1])
                    for chan, tgt in sorted(op.waits.items()):
                        eng.wait_ge(sems[chan], tgt)
                    inst = op.fn(eng)
                    if op.needs_inc:
                        if inst is None:
                            raise RuntimeError(f"op {op.tag or op.idx} needs inc, fn None")
                        inst.then_inc(sems[op.chan], op.inc_amt)
                if extra_guard:
                    for chan, sem in sems.items():
                        eng.wait_ge(sem, final_counts[chan])
            return body

        for e in ENGINES:
            getattr(block, eng_attr[e])(emit_engine(e, by_engine[e]))
        stack.close()
        return final_counts


B, S, D = 4, 4096, 1024
H, DH, M = 16, 64, 256
TOK = 2048
NT, TT, NST, NC8 = 4, 512, 4, 8
DN = float(DH) ** -0.25
KEPS = 1e-4
LNEPS = 1e-5
NBCH = 24
PAIRS = [[0, 1], [2, 3], [4, 5], [6, 7]]
ALL8 = [[0, 1, 2, 3, 4, 5, 6, 7]]
TW_PERM = [4, 5, 6, 7, 0, 1, 2, 3]
TW_SIGN = [1.0, 1.0, -1.0, -1.0, -1.0, -1.0, 1.0, 1.0]


def build_nc():
    nc = bass.Bass("TRN2", target_bir_lowering=False, debug=False)

    d_xT = nc.dram_tensor("xT", [D, TOK], F32, kind="ExternalInput").ap()
    d_out = nc.dram_tensor("oT", [D, TOK], F32, kind="ExternalOutput").ap()
    d_x1 = nc.dram_tensor("x1s", [D, TOK], F32, kind="Internal").ap()
    d_x2 = nc.dram_tensor("x2s", [D, TOK], F32, kind="Internal").ap()
    d_bch = nc.dram_tensor("bch", [NBCH, TT], F32, kind="Internal").ap()
    dw = {}
    for nm, shp in [("wq", [D, D]), ("wk", [D, D]), ("wv", [D, D]), ("wo", [D, D]),
                    ("proj", [M, DH]),
                    ("cn_w1", [D, D]), ("cn_w2", [D, 512]), ("cn_w3", [512, 256]),
                    ("gc_w1", [D, 512]), ("gc_w2", [512, 256])]:
        dw[nm] = nc.dram_tensor(nm, shp, F32, kind="ExternalInput").ap()
    for nm, n in [("bo", D), ("cn_b1", D), ("cn_b2", 512), ("cn_b3", 256),
                  ("gc_b1", 512), ("gc_b2", 256), ("cn_w4r", 256), ("gc_w3r", 256)]:
        dw[nm] = nc.dram_tensor(nm, [1, n], F32, kind="ExternalInput").ap()
    for nm in ("cn_b4", "gc_b3", "arange"):
        dw[nm] = nc.dram_tensor(nm, [1, 1], F32, kind="ExternalInput").ap()
    d_gin = [nc.dram_tensor(f"gin{i}", [128, NC8], F32, kind="Internal").ap() for i in range(2)]
    d_gout = [nc.dram_tensor(f"gout{i}", [128, NC8], F32, kind="Internal").ap() for i in range(2)]
    d_ctxin = nc.dram_tensor("ctxin", [8 * 65, TT], F32, kind="Internal").ap()
    d_ctxout = nc.dram_tensor("ctxout", [8 * 65, TT], F32, kind="Internal").ap()
    d_vsin = nc.dram_tensor("vsin", [1, D], F32, kind="Internal").ap()
    d_vsout = nc.dram_tensor("vsout", [1, D], F32, kind="Internal").ap()
    d_emin = nc.dram_tensor("emin", [1, 1], F32, kind="Internal").ap()
    d_emout = nc.dram_tensor("emout", [1, 1], F32, kind="Internal").ap()

    p = Prog(nc)
    gstack = ExitStack()
    GE = gstack.enter_context
    R = lambda *k: k

    g_ident = GE(nc.sbuf_tensor("ident", [128, 128], BF16))
    g_onesbr = GE(nc.sbuf_tensor("onesbr", [1, 128], BF16))
    g_onesfr = GE(nc.sbuf_tensor("onesfr", [1, 128], F32))
    g_onesbc = GE(nc.sbuf_tensor("onesbc", [128, 1], BF16))
    g_onesf1 = GE(nc.sbuf_tensor("onesf1", [1, 1], F32))
    g_ind2 = GE(nc.sbuf_tensor("ind2", [128, 2], BF16))
    g_projT2 = GE(nc.sbuf_tensor("projT2", [128, M], BF16))
    g_projblk = GE(nc.sbuf_tensor("projblk", [128, 2 * M], BF16))
    g_negU = GE(nc.sbuf_tensor("negU", [128, 1], F32))
    g_negUle = GE(nc.sbuf_tensor("negUle", [128, 1], F32))
    g_epsln = GE(nc.sbuf_tensor("epsln", [1, 1], F32))
    g_bo = GE(nc.sbuf_tensor("g_bo", [128, NC8], F32))
    g_b1 = GE(nc.sbuf_tensor("g_b1", [128, NC8], F32))
    g_b2 = GE(nc.sbuf_tensor("g_b2", [128, 4], F32))
    g_b3 = GE(nc.sbuf_tensor("g_b3", [128, 2], F32))
    g_gb1 = GE(nc.sbuf_tensor("g_gb1", [128, 4], F32))
    g_gb2 = GE(nc.sbuf_tensor("g_gb2", [128, 2], F32))
    g_w4 = GE(nc.sbuf_tensor("g_w4", [128, 2], BF16))
    g_gw3 = GE(nc.sbuf_tensor("g_gw3", [128, 2], BF16))
    g_halfb4 = GE(nc.sbuf_tensor("g_halfb4", [1, 1], F32))
    g_halfgb3 = GE(nc.sbuf_tensor("g_halfgb3", [1, 1], F32))
    g_ar = GE(nc.sbuf_tensor("g_ar", [1, 1], F32))
    g_sc = GE(nc.sbuf_tensor("g_sc", [1, 8], F32))
    g_w3m = GE(nc.sbuf_tensor("g_w3m", [128, 4, 256], BF16))
    g_gw1 = GE(nc.sbuf_tensor("g_gw1", [128, NC8, 512], BF16))
    g_gw2 = GE(nc.sbuf_tensor("g_gw2", [128, 4, 256], BF16))
    g_w2m = GE(nc.sbuf_tensor("g_w2m", [128, NC8, 512], BF16))

    def MM(out, lhsT, rhs, reads, writes, start, stop, tag=""):
        p.pe(lambda e, o=out, l=lhsT, r=rhs, s=start, sp_=stop:
             e.matmul(o, l, r, start=s, stop=sp_), reads, writes, tag=tag)

    def TRANS(out, in_, reads, writes, tag="", pdim=128):
        p.pe(lambda e, o=out, i=in_, pd=pdim: e.transpose(o, i, g_ident[0:pd, 0:pd]),
             list(reads) + [R("ident")], writes, tag=tag)

    def ACT(out, in_, func, reads, writes, bias=None, scale=1.0, tag=""):
        if bias is None:
            p.act(lambda e, o=out, i=in_, f=func, s=scale:
                  e.activation(o, i, f, scale=s), reads, writes, tag=tag)
        else:
            p.act(lambda e, o=out, i=in_, f=func, s=scale, b=bias:
                  e.activation(o, i, f, bias=b, scale=s), reads, writes, tag=tag)

    def CPY(out, in_, reads, writes, eng="dve", tag=""):
        p.emit(eng, lambda e, o=out, i=in_: e.tensor_copy(o, i), reads, writes, tag=tag)

    def ETT(op, out, a, b, reads, writes, eng="dve", tag=""):
        p.emit(eng, lambda e, o=out, x=a, y=b, v=op: e.tensor_tensor(o, x, y, v),
               reads, writes, tag=tag)

    def TS(out, in_, s1, s2, op0, op1, reads, writes, accum=None, eng="dve", tag=""):
        p.emit(eng, lambda e, o=out, i=in_, a=s1, b=s2, o0=op0, o1=op1, ac=accum:
               e.tensor_scalar(o, i, a, b, op0=o0, op1=o1, accum_out=ac),
               reads, writes, tag=tag)

    def STT(out, in0, scl, in1, op0, op1, reads, writes, accum=None, eng="dve", tag=""):
        p.emit(eng, lambda e, o=out, i0=in0, s=scl, i1=in1, o0=op0, o1=op1, ac=accum:
               e.scalar_tensor_tensor(o, i0, s, i1, op0=o0, op1=o1, accum_out=ac),
               reads, writes, tag=tag)

    def RED(kind, out, in_, reads, writes, tag=""):
        if kind == "max":
            p.dve(lambda e, o=out, i=in_: e.reduce_max(o, i, axis=AXX), reads, writes, tag=tag)
        else:
            p.dve(lambda e, o=out, i=in_: e.reduce_sum(o, i, axis=AXX), reads, writes, tag=tag)

    def MSET(ap_, v, reads, writes, eng="dve", tag=""):
        p.emit(eng, lambda e, a=ap_, x=v: e.memset(a, x), reads, writes, tag=tag)

    bcslot = [0]

    def BCAST(row_ap, row_reads, dst_ap, dst_reg, npart, tag=""):
        s = bcslot[0] % 16
        bcslot[0] += 1
        p.dma("sp", d_bch[s:s + 1, :], row_ap, list(row_reads), [R("bch", s)],
              tag=tag + "o")
        src = d_bch[s:s + 1, 0:TT].squeeze(0).partition_broadcast(npart)
        p.dma("sp", dst_ap, src, [R("bch", s)], [dst_reg], tag=tag + "i")

    def LN_STATS(tg, xap, xreg, xb, pr0, pr0reg, pr1, pr1reg, stat, pi,
                 bc0ap, bc0reg, bc1ap, bc1reg, t, cast_eng="act"):
        for c in range(NC8):
            xbc = xb[:, c % 2, :]
            xbr = R(tg, "xb", c % 2)
            if cast_eng == "act":
                ACT(xbc, xap(c), AF.Copy, [xreg(c)], [xbr], tag=tg + f"cst{t}{c}")
            else:
                CPY(xbc, xap(c), [xreg(c)], [xbr], eng=cast_eng, tag=tg + f"cst{t}{c}")
            MM(pr0, g_onesbc[:, :], xbc, [xbr, R("onesbc")], [pr0reg],
               c == 0, c == NC8 - 1, tag=tg + f"sm{t}{c}")
            ETT(ALU.mult, xbc, xbc, xbc, [xbr], [xbr], tag=tg + f"sqm{t}{c}")
            MM(pr1, g_onesbc[:, :], xbc, [xbr, R("onesbc")], [pr1reg],
               c == 0, c == NC8 - 1, tag=tg + f"sq{t}{c}")
        mean, var = stat[:, 0:TT], stat[:, TT:2 * TT]
        ACT(mean, pr0, AF.Identity, [pr0reg], [R(tg, "st0", pi)], scale=1.0 / D)
        ETT(ALU.mult, var, mean, mean, [R(tg, "st0", pi)], [R(tg, "st1", pi)])
        STT(var, pr1, 1.0 / D, var, ALU.mult, ALU.subtract,
            [pr1reg, R(tg, "st1", pi)], [R(tg, "st1", pi)], tag=tg + f"var{t}")
        ACT(var, var, AF.Ln, [R(tg, "st1", pi), R("epsln")], [R(tg, "st1", pi)],
            bias=g_epsln[:, :])
        ACT(var, var, AF.Exp, [R(tg, "st1", pi)], [R(tg, "st1", pi)], scale=-0.5)
        STT(mean, mean, -1.0, var, ALU.mult, ALU.mult,
            [R(tg, "st0", pi), R(tg, "st1", pi)], [R(tg, "st0", pi)])
        BCAST(var, [R(tg, "st1", pi)], bc0ap, bc0reg, 128, tag=tg + f"b0_{t}")
        BCAST(mean, [R(tg, "st0", pi)], bc1ap, bc1reg, 128, tag=tg + f"b1_{t}")

    with nc.psum_tensor("pi_t", [128, 128], F32) as pi_t, \
         nc.psum_tensor("pi_b", [128, 128], BF16) as pi_b, \
         nc.sbuf_tensor("i_row", [1, 8, D], F32) as i_rows, \
         nc.sbuf_tensor("i_rowb", [1, 128], BF16) as i_rowb, \
         nc.sbuf_tensor("i_proj", [128, DH], F32) as i_proj, \
         nc.sbuf_tensor("i_projb", [128, DH], BF16) as i_projb, \
         nc.sbuf_tensor("i_ps", [128, DH], F32) as i_ps, \
         nc.sbuf_tensor("i_col", [128, 2], F32) as i_col, \
         nc.sbuf_tensor("i_colb", [128, 1], BF16) as i_colb:

        MSET(g_ident[:, :], 0.0, [], [R("ident")], eng="pool")
        p.pool(lambda e: e.affine_select(
            out=g_ident[:, :], in_=g_ident[:, :], compare_op=ALU.not_equal,
            fill=1.0, base=0, pattern=[[-1, 128]], channel_multiplier=1),
            [R("ident")], [R("ident")], tag="identsel")
        MSET(g_onesbr[:, :], 1.0, [], [R("onesbr")])
        MSET(g_onesfr[:, :], 1.0, [], [R("onesfr")])
        MSET(g_onesbc[:, :], 1.0, [], [R("onesbc")])
        MSET(g_onesf1[:, :], 1.0, [], [R("onesf1")])
        MSET(g_ind2[:, :], 0.0, [], [R("ind2")])
        MSET(g_ind2[0:64, 0:1], 0.5, [R("ind2")], [R("ind2")])
        MSET(g_ind2[64:128, 1:2], 0.5, [R("ind2")], [R("ind2")])
        MSET(g_epsln[:, :], LNEPS, [], [R("epsln")])

        for c in range(2):
            p.dma("pool", i_proj[:, :], dw["proj"][128 * c:128 * (c + 1), :],
                  [], [R("iproj")], tag=f"ld_proj{c}")
            CPY(i_projb[:, :], i_proj[:, :], [R("iproj")], [R("iprojb")])
            TRANS(pi_b[0:DH, 0:128], i_projb[:, :], [R("iprojb")], [R("pib")], tag=f"trp{c}")
            CPY(g_projT2[0:DH, 128 * c:128 * (c + 1)], pi_b[0:DH, 0:128],
                [R("pib")], [R("projT2")])
            ETT(ALU.mult, i_ps[:, :], i_proj[:, :], i_proj[:, :], [R("iproj")], [R("ips")])
            RED("sum", i_col[:, c:c + 1], i_ps[:, :], [R("ips")], [R("icol")])
        CPY(g_projT2[64:128, :], g_projT2[0:64, :], [R("projT2")], [R("projT2")])
        MSET(g_projblk[:, :], 0.0, [], [R("projblk")])
        CPY(g_projblk[0:64, 0:M], g_projT2[0:64, :], [R("projT2"), R("projblk")], [R("projblk")])
        CPY(g_projblk[64:128, M:2 * M], g_projT2[0:64, :], [R("projT2"), R("projblk")], [R("projblk")])
        ETT(ALU.max, i_col[:, 0:1], i_col[:, 0:1], i_col[:, 1:2], [R("icol")], [R("icol")])
        CPY(i_colb[:, :], i_col[:, 0:1], [R("icol")], [R("icolb")])
        TRANS(pi_b[0:1, 0:128], i_colb[:, :], [R("icolb")], [R("pib")], tag="tru")
        RED("max", g_sc[0:1, 0:1], pi_b[0:1, 0:128], [R("pib")], [R("sc0")], tag="umax")
        ACT(g_sc[0:1, 1:2], g_sc[0:1, 0:1], AF.Identity, [R("sc0")], [R("sc1")], scale=0.505)
        MM(pi_t[:, 0:1], g_onesfr[:, :], g_sc[0:1, 1:2], [R("onesfr"), R("sc1")],
           [R("pit")], True, True, tag="bcU")
        ACT(g_negU[:, :], pi_t[:, 0:1], AF.Identity, [R("pit")], [R("negU")], scale=-1.0)
        TS(g_negUle[:, :], g_negU[:, :], math.log(KEPS), None, ALU.add, ALU.bypass,
           [R("negU")], [R("negUle")], tag="negUle")

        bias_list = [("bo", 8), ("cn_b1", 8), ("cn_b2", 4), ("cn_b3", 2),
                     ("gc_b1", 4), ("gc_b2", 2)]
        for bi, (nm, nchunks) in enumerate(bias_list):
            p.dma("pool", i_rows[:, bi, 0:128 * nchunks], dw[nm][0:1, :],
                  [], [R("irow", nm)], tag=f"ld_{nm}")

        def bias_cols_f32(bi, nm, dst, nchunks, dstreg):
            for c in range(nchunks):
                MM(pi_t[:, c:c + 1], i_rows[:, bi, 128 * c:128 * (c + 1)],
                   g_onesf1[:, :],
                   [R("irow", nm), R("onesf1")], [R("pit")], True, True, tag=f"tr_{nm}{c}")
            for c in range(nchunks):
                CPY(dst[:, c:c + 1], pi_t[:, c:c + 1], [R("pit")], [R(dstreg)])

        bias_cols_f32(0, "bo", g_bo, NC8, "g_bo")
        bias_cols_f32(1, "cn_b1", g_b1, NC8, "g_b1")
        bias_cols_f32(2, "cn_b2", g_b2, 4, "g_b2")
        bias_cols_f32(3, "cn_b3", g_b3, 2, "g_b3")
        bias_cols_f32(4, "gc_b1", g_gb1, 4, "g_gb1")
        bias_cols_f32(5, "gc_b2", g_gb2, 2, "g_gb2")

        def wcol_bf(bi, nm, dst, dstreg):
            p.dma("pool", i_rows[:, bi, 0:256], dw[nm][0:1, :],
                  [], [R("irow", nm)], tag=f"ld_{nm}")
            for c in range(2):
                CPY(i_rowb[:, :], i_rows[:, bi, 128 * c:128 * (c + 1)],
                    [R("irow", nm)], [R("irowb")])
                TRANS(pi_b[0:128, 0:1], i_rowb[:, :], [R("irowb")], [R("pib")], tag=f"tw_{nm}{c}", pdim=1)
                CPY(dst[:, c:c + 1], pi_b[0:128, 0:1], [R("pib")], [R(dstreg)])

        wcol_bf(6, "cn_w4r", g_w4, "g_w4")
        wcol_bf(7, "gc_w3r", g_gw3, "g_gw3")

        for c in range(NC8):
            p.dma("pool", g_gw1[:, c, :], dw["gc_w1"][128 * c:128 * (c + 1), :],
                  [], [R("g_gw1", c)], tag=f"ld_gw1{c}")
            p.dma("pool", g_w2m[:, c, :], dw["cn_w2"][128 * c:128 * (c + 1), :],
                  [], [R("g_w2m", c)], tag=f"ld_w2{c}")
        for c in range(4):
            p.dma("pool", g_w3m[:, c, :], dw["cn_w3"][128 * c:128 * (c + 1), :],
                  [], [R("g_w3m", c)], tag=f"ld_w3{c}")
            p.dma("pool", g_gw2[:, c, :], dw["gc_w2"][128 * c:128 * (c + 1), :],
                  [], [R("g_gw2", c)], tag=f"ld_gw2{c}")
        p.dma("pool", g_ar[:, :], dw["arange"], [], [R("ar")], tag="ld_ar")
        p.dma("pool", g_sc[0:1, 2:3], dw["cn_b4"], [], [R("b4r")], tag="ld_b4")
        p.dma("pool", g_sc[0:1, 3:4], dw["gc_b3"], [], [R("gb3r")], tag="ld_gb3")
        ACT(g_halfb4[:, :], g_sc[0:1, 2:3], AF.Identity, [R("b4r")], [R("halfb4")], scale=0.5)
        ACT(g_halfgb3[:, :], g_sc[0:1, 3:4], AF.Identity, [R("gb3r")], [R("halfgb3")], scale=0.5)

    p.barrier()

    def emit_ffn(cid, xin, xout, inpre, outpre):
        tg = f"f{cid}_"
        with ExitStack() as fs:
            FE = fs.enter_context
            w1 = FE(nc.sbuf_tensor(tg + "w1", [128, NC8, D], BF16))
            y = FE(nc.sbuf_tensor(tg + "y", [128, NC8, TOK], F32))
            ybf = FE(nc.sbuf_tensor(tg + "ybf", [128, NT, NC8, TT], BF16))
            xt = FE(nc.sbuf_tensor(tg + "xt", [128, 2, NC8, TT], F32))
            xb = FE(nc.sbuf_tensor(tg + "xb", [128, 2, TT], BF16))
            h1 = FE(nc.sbuf_tensor(tg + "h1", [128, 2, TT], BF16))
            h2 = FE(nc.sbuf_tensor(tg + "h2", [128, 4, TT], BF16))
            h3 = FE(nc.sbuf_tensor(tg + "h3", [128, 3, 2, TT], BF16))
            gsum = FE(nc.sbuf_tensor(tg + "gsum", [128, NC8 * NT], F32))
            gsumr = FE(nc.sbuf_tensor(tg + "gsumr", [128, NC8], F32))
            stat = FE(nc.sbuf_tensor(tg + "stat", [1, 2, 2 * TT], F32))
            gnet = FE(nc.sbuf_tensor(tg + "gnet", [128, 16], BF16))
            scs = FE(nc.sbuf_tensor(tg + "scs", [1, 8], F32))
            tmp8 = FE(nc.sbuf_tensor(tg + "tmp8", [128, 3, TT], F32))
            bc0 = FE(nc.sbuf_tensor(tg + "bc0", [128, 2, TT], F32))
            bc1 = FE(nc.sbuf_tensor(tg + "bc1", [128, 2, TT], F32))
            cbst = FE(nc.sbuf_tensor(tg + "cbst", [128, 2, TT], F32))
            pA = [FE(nc.psum_tensor(tg + f"pA{i}", [128, TT], F32)) for i in range(2)]
            pB = [FE(nc.psum_tensor(tg + f"pB{i}", [128, TT], F32)) for i in range(4)]
            pW = [FE(nc.psum_tensor(tg + f"pW{i}", [128, TT], F32)) for i in range(2)]

            def load_w1():
                for c in range(NC8):
                    p.dma("pool", w1[:, c, :], dw["cn_w1"][128 * c:128 * (c + 1), :],
                          [], [R(tg, "w1", c)], tag=tg + f"w1{c}")


            def L1(t):
                pi = t % 2
                for c in range(NC8):
                    p.dma("sp", xt[:, pi, c, :], xin[128 * c:128 * (c + 1), TT * t:TT * (t + 1)],
                          [R(inpre, c, t)], [R(tg, "xt", pi, c)], tag=tg + f"lx{t}_{c}")
                LN_STATS(tg, lambda c: xt[:, pi, c, :], lambda c: R(tg, "xt", pi, c), xb,
                         pW[0][0:1, 0:TT], R(tg, "pW", 0),
                         pW[1][0:1, 0:TT], R(tg, "pW", 1),
                         stat[:, pi, :], pi,
                         bc0[:, pi, :], R(tg, "bc0", pi),
                         bc1[:, pi, :], R(tg, "bc1", pi), t)

            def BCIN(t):
                pi = t % 2
                s0 = d_bch[16 + 2 * t:17 + 2 * t, 0:TT].squeeze(0).partition_broadcast(128)
                p.dma("sp", bc0[:, pi, :], s0, [R("bch", 16 + 2 * t)],
                      [R(tg, "bc0", pi)], tag=tg + f"bi0{t}")
                s1 = d_bch[17 + 2 * t:18 + 2 * t, 0:TT].squeeze(0).partition_broadcast(128)
                p.dma("sp", bc1[:, pi, :], s1, [R("bch", 17 + 2 * t)],
                      [R(tg, "bc1", pi)], tag=tg + f"bi1{t}")

            def L2(t):
                pi = t % 2
                if cid == 2:
                    for c in range(NC8):
                        p.dma("sp", xt[:, pi, c, :],
                              xin[128 * c:128 * (c + 1), TT * t:TT * (t + 1)],
                              [R(inpre, c, t)], [R(tg, "xt", pi, c)], tag=tg + f"lx{t}_{c}")
                for c in range(NC8):
                    s = c % 3
                    ETT(ALU.mult, tmp8[:, s, :], xt[:, pi, c, :], bc0[:, pi, :],
                        [R(tg, "xt", pi, c), R(tg, "bc0", pi)], [R(tg, "tmp8", s)],
                        eng=("pool" if c % 2 == 0 else "dve"), tag=tg + f"lnm{t}{c}")
                    STT(y[:, c, TT * t:TT * (t + 1)], bc1[:, pi, :], 1.0, tmp8[:, s, :],
                        ALU.bypass, ALU.add,
                        [R(tg, "bc1", pi), R(tg, "tmp8", s)],
                        [R(tg, "y", c, t), R(tg, "gsum", c, t)],
                        accum=gsum[:, NT * c + t:NT * c + t + 1], tag=tg + f"ln{t}_{c}")
                    CPY(ybf[:, t, c, :], y[:, c, TT * t:TT * (t + 1)],
                        [R(tg, "y", c, t)], [R(tg, "ybf", t, c)], tag=tg + f"yc{t}{c}")

            def UW(u):
                cyc, t = divmod(u, NT)
                for m in range(NC8 + 1):
                    if m < NC8:
                        pa, preg = pA[m % 2], R(tg, "pA", m % 2)
                        for k in range(NC8):
                            MM(pa[:, :], w1[:, k, 128 * m:128 * (m + 1)], ybf[:, t, k, :],
                               [R(tg, "w1", k), R(tg, "ybf", t, k)], [preg],
                               k == 0, k == NC8 - 1, tag=tg + f"W1_{u}{m}{k}")
                        ACT(h1[:, m % 2, :], pa[:, :], AF.Gelu, [preg, R("g_b1")],
                            [R(tg, "h1", m % 2)], bias=g_b1[:, m:m + 1],
                            tag=tg + f"H1_{u}{m}")
                    if m >= 1:
                        mm_ = m - 1
                        for m2 in range(4):
                            MM(pB[m2][:, :], g_w2m[:, mm_, 128 * m2:128 * (m2 + 1)],
                               h1[:, mm_ % 2, :],
                               [R("g_w2m", mm_), R(tg, "h1", mm_ % 2)], [R(tg, "pB", m2)],
                               mm_ == 0, mm_ == NC8 - 1, tag=tg + f"W2_{u}{m2}{mm_}")
                            if mm_ == NC8 - 1:
                                ACT(h2[:, m2, :], pB[m2][:, :], AF.Gelu,
                                    [R(tg, "pB", m2), R("g_b2")], [R(tg, "h2", m2)],
                                    bias=g_b2[:, m2:m2 + 1], tag=tg + f"H2_{u}{m2}")

            def T1(u):
                pi3 = u % 3
                for m2 in range(4):
                    for m3 in range(2):
                        MM(pW[m3][:, :], g_w3m[:, m2, 128 * m3:128 * (m3 + 1)], h2[:, m2, :],
                           [R("g_w3m", m2), R(tg, "h2", m2)], [R(tg, "pW", m3)],
                           m2 == 0, m2 == 3, tag=tg + f"W3_{u}{m3}{m2}")
                for m3 in range(2):
                    ACT(h3[:, pi3, m3, :], pW[m3][:, :], AF.Gelu,
                        [R(tg, "pW", m3), R("g_b3")], [R(tg, "h3", pi3, m3)],
                        bias=g_b3[:, m3:m3 + 1], tag=tg + f"H3_{u}{m3}")

            def crowof(u):
                pi2, hf = ((0, 0), (1, 0), (0, 1))[u % 3]
                return (stat[:, pi2, hf * TT:(hf + 1) * TT],
                        R(tg, "st0" if hf == 0 else "st1", pi2))

            def T2a(u):
                pi3 = u % 3
                crow, crreg = crowof(u)
                for m3 in range(2):
                    MM(pB[0][0:1, 0:TT], g_w4[:, m3:m3 + 1], h3[:, pi3, m3, :],
                       [R("g_w4"), R(tg, "h3", pi3, m3)], [R(tg, "pB", 0)],
                       m3 == 0, m3 == 1, tag=tg + f"W4_{u}{m3}")
                ACT(crow, pB[0][0:1, 0:TT], AF.Tanh,
                    [R(tg, "pB", 0), R("halfb4")], [crreg],
                    scale=0.5, bias=g_halfb4[:, :], tag=tg + f"TF_{u}")

            def T2b(u):
                cyc, t = divmod(u, NT)
                pi = u % 2
                last = cyc == 2
                crow, crreg = crowof(u)
                if last:
                    ca_ap, cb_ap = scs[:, 4:5], scs[:, 5:6]
                    ca_reg, cb_reg = R(tg, "ca3"), R(tg, "cb3")
                else:
                    ca_ap, cb_ap = scs[:, 2:3], scs[:, 3:4]
                    ca_reg, cb_reg = R(tg, "ca"), R(tg, "cb")
                TS(crow, crow, cb_ap, ca_ap,
                   ALU.mult, ALU.add,
                   [crreg, ca_reg, cb_reg], [crreg],
                   tag=tg + f"CP_{u}")
                BCAST(crow, [crreg],
                      cbst[:, pi, :], R(tg, "cbst", pi), 128, tag=tg + f"BC_{u}")

            def T3(u):
                cyc, t = divmod(u, NT)
                pi = u % 2
                if cyc < 2:
                    for pr in range(4):
                        cA, cB = pr, pr + 4
                        sA, sB = (2 * pr) % 3, (2 * pr + 1) % 3
                        for c, s in ((cA, sA), (cB, sB)):
                            ETT(ALU.mult, tmp8[:, s, :], cbst[:, pi, :],
                                y[:, TW_PERM[c], TT * t:TT * (t + 1)],
                                [R(tg, "cbst", pi), R(tg, "y", TW_PERM[c], t)],
                                [R(tg, "tmp8", s)], eng="dve", tag=tg + f"tw{u}{c}")
                        for c, s in ((cA, sA), (cB, sB)):
                            sgn = ALU.add if TW_SIGN[c] > 0 else ALU.subtract
                            ya = y[:, c, TT * t:TT * (t + 1)]
                            ETT(sgn, ybf[:, t, c, :], ya, tmp8[:, s, :],
                                [R(tg, "y", c, t), R(tg, "tmp8", s)],
                                [R(tg, "ybf", t, c)], eng="dve", tag=tg + f"yb{u}{c}")
                            ETT(sgn, ya, ya, tmp8[:, s, :],
                                [R(tg, "y", c, t), R(tg, "tmp8", s)], [R(tg, "y", c, t)],
                                eng="pool", tag=tg + f"yu{u}{c}")
                else:
                    pt = t % 2
                    for c in range(NC8):
                        p.dma("sp", xt[:, pt, c, :],
                              xin[128 * c:128 * (c + 1), TT * t:TT * (t + 1)],
                              [R(inpre, c, t)], [R(tg, "xt", pt, c)], tag=tg + f"lq{t}_{c}")
                    for c in range(NC8):
                        ya = y[:, c, TT * t:TT * (t + 1)]
                        ypa = y[:, TW_PERM[c], TT * t:TT * (t + 1)]
                        sw = c % 3
                        sgn = ALU.add if TW_SIGN[c] > 0 else ALU.subtract
                        ETT(ALU.mult, tmp8[:, sw, :], cbst[:, pi, :], ypa,
                            [R(tg, "cbst", pi), R(tg, "y", TW_PERM[c], t)],
                            [R(tg, "tmp8", sw)], eng="dve", tag=tg + f"tw3{t}{c}")
                        STT(xt[:, pt, c, :], ya, 0.5, xt[:, pt, c, :], ALU.mult, ALU.add,
                            [R(tg, "y", c, t), R(tg, "xt", pt, c)], [R(tg, "xt", pt, c)],
                            tag=tg + f"xh{t}{c}")
                        ETT(sgn, xt[:, pt, c, :], xt[:, pt, c, :], tmp8[:, sw, :],
                            [R(tg, "xt", pt, c), R(tg, "tmp8", sw)], [R(tg, "xt", pt, c)],
                            eng="pool", tag=tg + f"xo{t}{c}")
                        p.dma("sp", xout[128 * c:128 * (c + 1), TT * t:TT * (t + 1)],
                              xt[:, pt, c, :], [R(tg, "xt", pt, c)], [R(outpre, c, t)],
                              tag=tg + f"st{t}{c}")

            def gate_launch():
                for c in range(NC8):
                    RED("sum", gsumr[:, c:c + 1], gsum[:, NT * c:NT * (c + 1)],
                        [R(tg, "gsum", c, tt) for tt in range(NT)], [R(tg, "gsumr")],
                        tag=tg + f"gr{c}")
                gi, go = d_gin[cid - 1], d_gout[cid - 1]
                p.dma("sp", gi, gsumr[:, :], [R(tg, "gsumr")], [R(tg, "gi")], tag=tg + "gs")
                p.emit("pool", lambda e, gi=gi, go=go: e.collective_compute(
                    "AllReduce", ALU.add, replica_groups=PAIRS, ins=[gi], outs=[go]),
                    [R(tg, "gi")], [R(tg, "go")], cc=True, tag=tg + "gar")
                p.dma("sp", gsumr[:, :], go, [R(tg, "go")], [R(tg, "gsumr")], tag=tg + "gb")

            def gate_net():
                ACT(gnet[:, 0:NC8], gsumr[:, :], AF.Identity, [R(tg, "gsumr")],
                    [R(tg, "gctx")], scale=1.0 / S)
                for m in range(4):
                    for k in range(NC8):
                        MM(pW[0][:, m:m + 1], g_gw1[:, k, 128 * m:128 * (m + 1)], gnet[:, k:k + 1],
                           [R("g_gw1", k), R(tg, "gctx")], [R(tg, "pW", 0)],
                           k == 0, k == NC8 - 1, tag=tg + f"g1_{m}_{k}")
                    ACT(gnet[:, 8 + m:9 + m], pW[0][:, m:m + 1], AF.Gelu,
                        [R(tg, "pW", 0), R("g_gb1")], [R(tg, "g1", m)],
                        bias=g_gb1[:, m:m + 1], tag=tg + f"g1a{m}")
                for m in range(2):
                    for k in range(4):
                        MM(pW[1][:, m:m + 1], g_gw2[:, k, 128 * m:128 * (m + 1)], gnet[:, 8 + k:9 + k],
                           [R("g_gw2", k), R(tg, "g1", k)], [R(tg, "pW", 1)],
                           k == 0, k == 3, tag=tg + f"g2_{m}_{k}")
                    ACT(gnet[:, 12 + m:13 + m], pW[1][:, m:m + 1], AF.Gelu,
                        [R(tg, "pW", 1), R("g_gb2")], [R(tg, "g2", m)],
                        bias=g_gb2[:, m:m + 1], tag=tg + f"g2a{m}")
                for k in range(2):
                    MM(pB[1][0:1, 0:1], g_gw3[:, k:k + 1], gnet[:, 12 + k:13 + k],
                       [R("g_gw3"), R(tg, "g2", k)], [R(tg, "pB", 1)], k == 0, k == 1,
                       tag=tg + f"g3_{k}")
                ACT(scs[:, 0:1], pB[1][0:1, 0:1], AF.Tanh, [R(tg, "pB", 1), R("halfgb3")],
                    [R(tg, "tgf")], scale=0.5, bias=g_halfgb3[:, :], tag=tg + "gftanh")
                ETT(ALU.mult, scs[:, 1:2], scs[:, 0:1], g_ar[:, :], [R(tg, "tgf"), R("ar")],
                    [R(tg, "t1")])
                TS(scs[:, 2:3], scs[:, 1:2], 0.7, 0.1, ALU.mult, ALU.add,
                   [R(tg, "t1")], [R(tg, "ca")], tag=tg + "ca")
                TS(scs[:, 3:4], g_ar[:, :], 0.3, None, ALU.mult, ALU.bypass,
                   [R("ar")], [R(tg, "cb")], tag=tg + "cb")
                TS(scs[:, 4:5], scs[:, 2:3], 0.5, None, ALU.mult, ALU.bypass,
                   [R(tg, "ca")], [R(tg, "ca3")])
                TS(scs[:, 5:6], scs[:, 3:4], 0.5, None, ALU.mult, ALU.bypass,
                   [R(tg, "cb")], [R(tg, "cb3")])

            load_w1()
            if cid == 1:
                L1(0)
                L1(1)
            else:
                BCIN(0)
                BCIN(1)
            L2(0)
            L2(1)
            UW(0); T1(0)
            L1(2) if cid == 1 else BCIN(2)
            L2(2)
            UW(1); T1(1); T2a(0)
            L1(3) if cid == 1 else BCIN(3)
            L2(3)
            gate_launch()
            UW(2); T1(2); T2a(1)
            UW(3); T1(3); T2a(2)
            gate_net()
            T2b(0); T3(0); T2b(1)
            for u in range(4, 12):
                UW(u); T1(u); T2a(u - 1)
                T3(u - 3); T2b(u - 2)
            T2a(11)
            T3(9); T2b(10)
            T3(10); T2b(11)
            T3(11)

    def emit_attn():
        tg = "a_"
        with ExitStack() as fs:
            FE = fs.enter_context
            qh = FE(nc.sbuf_tensor(tg + "qh", [128, NC8, TOK], BF16))
            ctxs = FE(nc.sbuf_tensor(tg + "ctxs", [65, NC8, TT], F32))
            gTb = FE(nc.sbuf_tensor(tg + "gTb", [16, NT * TT], BF16))
            rmx = FE(nc.sbuf_tensor(tg + "rmx", [128, NST, H], F32))
            ndiag = FE(nc.sbuf_tensor(tg + "ndiag", [128, NST, H], F32))
            edia = FE(nc.sbuf_tensor(tg + "edia", [128, NST, H], F32))
            mrun = FE(nc.sbuf_tensor(tg + "mrun", [128, H], F32))
            mcolb = FE(nc.sbuf_tensor(tg + "mcolb", [128, 1], BF16))
            tmpA = FE(nc.sbuf_tensor(tg + "tmpA", [128, 3, TT], F32))
            bc0 = FE(nc.sbuf_tensor(tg + "bc0", [128, 2, TT], F32))
            bc1 = FE(nc.sbuf_tensor(tg + "bc1", [128, 2, TT], F32))
            xt = FE(nc.sbuf_tensor(tg + "xt", [128, NC8, TT], F32))
            stat = FE(nc.sbuf_tensor(tg + "stat", [1, 2, 2 * TT], F32))
            xb = FE(nc.sbuf_tensor(tg + "xb", [128, 2, TT], BF16))
            gin_ = FE(nc.sbuf_tensor(tg + "gin", [128, 16], F32))
            sq128 = FE(nc.sbuf_tensor(tg + "sq128", [128, 16], BF16))
            vsum = FE(nc.sbuf_tensor(tg + "vsum", [1, D], F32))
            scs = FE(nc.sbuf_tensor(tg + "scs", [1, 8], F32))
            ksq = FE(nc.sbuf_tensor(tg + "ksq", [128, 2, TT], BF16))
            mscr = FE(nc.sbuf_tensor(tg + "mscr", [128, M], BF16))
            ps_a = [FE(nc.psum_tensor(tg + f"psa{i}", [128, TT], F32)) for i in range(2)]
            ps_b = [FE(nc.psum_tensor(tg + f"psb{i}", [65, TT], F32)) for i in range(2)]
            ps_c = FE(nc.psum_tensor(tg + "psc", [128, TT], F32))
            ps_d = [FE(nc.psum_tensor(tg + f"psd{i}", [128, TT], F32)) for i in range(2)]
            ps_e = FE(nc.psum_tensor(tg + "pse", [128, 128], BF16))

            def a_stats(t):
                pi = t % 2
                for c in range(NC8):
                    p.dma("sp", xt[:, c, :], d_x1[128 * c:128 * (c + 1), TT * t:TT * (t + 1)],
                          [R("dx1", c, t)], [R(tg, "xt", c)], tag=tg + f"lx{t}_{c}")
                LN_STATS(tg, lambda c: xt[:, c, :], lambda c: R(tg, "xt", c), xb,
                         ps_d[0][0:1, 0:TT], R(tg, "psd", 0),
                         ps_d[1][0:1, 0:TT], R(tg, "psd", 1),
                         stat[:, pi, :], pi,
                         bc0[:, pi, :], R(tg, "bc0", pi),
                         bc1[:, pi, :], R(tg, "bc1", pi), t)

            def a_bcin(t):
                pi = t % 2
                s0 = d_bch[16 + 2 * t:17 + 2 * t, 0:TT].squeeze(0).partition_broadcast(128)
                p.dma("sp", bc0[:, pi, :], s0, [R("bch", 16 + 2 * t)],
                      [R(tg, "bc0", pi)], tag=tg + f"bi0{t}")
                s1 = d_bch[17 + 2 * t:18 + 2 * t, 0:TT].squeeze(0).partition_broadcast(128)
                p.dma("sp", bc1[:, pi, :], s1, [R("bch", 17 + 2 * t)],
                      [R(tg, "bc1", pi)], tag=tg + f"bi1{t}")

            with ExitStack() as fa:
                FA = fa.enter_context
                wq = FA(nc.sbuf_tensor(tg + "wq", [128, NC8, D], BF16))
                wk = FA(nc.sbuf_tensor(tg + "wk", [128, NC8, D], BF16))
                wv = FA(nc.sbuf_tensor(tg + "wv", [128, NC8, D], BF16))
                ln1 = FA(nc.sbuf_tensor(tg + "ln1", [128, 2, NC8, TT], BF16))
                kb = FA(nc.sbuf_tensor(tg + "kb", [128, NC8, TT], BF16))
                vtm = FA(nc.sbuf_tensor(tg + "vtm", [128, NST, H, 65], BF16))
                ek = FA(nc.sbuf_tensor(tg + "ek", [128, NST, 2 * M], BF16))

                MSET(mrun[:, :], 0.0, [], [R(tg, "mrun")])
                MSET(vsum[:, :], 0.0, [], [R(tg, "vsum")])
                MSET(vtm[:, :, :, :], 1.0, [], [R(tg, "vtm1")], eng="pool")

                def load_wqkv():
                    for c in range(NC8):
                        for wnm, wt in (("wq", wq), ("wk", wk), ("wv", wv)):
                            p.dma("pool", wt[:, c, :], dw[wnm][128 * c:128 * (c + 1), :],
                                  [], [R(tg, wnm, c)], tag=tg + f"{wnm}{c}")

                def a_apply(t):
                    pi = t % 2
                    for c in range(NC8):
                        s = c % 3
                        ETT(ALU.mult, tmpA[:, s, :], xt[:, c, :], bc0[:, pi, :],
                            [R(tg, "xt", c), R(tg, "bc0", pi)], [R(tg, "tmpA", s)],
                            eng=("pool" if c % 2 == 0 else "dve"), tag=tg + f"lnm{t}{c}")
                        ETT(ALU.add, ln1[:, pi, c, :], tmpA[:, s, :], bc1[:, pi, :],
                            [R(tg, "bc1", pi), R(tg, "tmpA", s)], [R(tg, "ln1", pi, c)],
                            eng=("dve" if c % 2 == 0 else "pool"), tag=tg + f"ln{t}{c}")

                a_stats(0)
                load_wqkv()
                a_apply(0)
                for t in range(NT):
                    pi = t % 2
                    for m in range(NC8):
                        pa, preg = ps_a[m % 2], R(tg, "psa", m % 2)
                        for k in range(NC8):
                            MM(pa[:, :], wq[:, k, 128 * m:128 * (m + 1)], ln1[:, pi, k, :],
                               [R(tg, "wq", k), R(tg, "ln1", pi, k)], [preg],
                               k == 0, k == NC8 - 1, tag=tg + f"Q{t}{m}{k}")
                        ACT(qh[:, m, TT * t:TT * (t + 1)], pa[:, :], AF.Identity, [preg],
                            [R(tg, "qh", m, t)], scale=DN, tag=tg + f"Qe{t}{m}")
                    for m in range(NC8):
                        pa, preg = ps_a[m % 2], R(tg, "psa", m % 2)
                        for k in range(NC8):
                            MM(pa[:, :], wk[:, k, 128 * m:128 * (m + 1)], ln1[:, pi, k, :],
                               [R(tg, "wk", k), R(tg, "ln1", pi, k)], [preg],
                               k == 0, k == NC8 - 1, tag=tg + f"K{t}{m}{k}")
                        ACT(kb[:, m, :], pa[:, :], AF.Identity, [preg],
                            [R(tg, "kb", m)], scale=DN, tag=tg + f"Ke{t}{m}")
                    for st in range(NST):
                        for half in range(2):
                            pa, preg = ps_a[(2 * st + half) % 2], R(tg, "psa", (2 * st + half) % 2)
                            for k in range(NC8):
                                MM(pa[:, :], ln1[:, pi, k, 128 * st:128 * (st + 1)],
                                   wv[:, k, 512 * half:512 * (half + 1)],
                                   [R(tg, "wv", k), R(tg, "ln1", pi, k)], [preg],
                                   k == 0, k == NC8 - 1, tag=tg + f"V{t}{st}{half}{k}")
                            CPY(vtm[:, st, 8 * half:8 * (half + 1), 0:64], pa[:, :],
                                [preg, R(tg, "vtm1")], [R(tg, "vtm", st, half)],
                                tag=tg + f"Ve{t}{st}{half}")
                    for c in range(NC8):
                        ETT(ALU.mult, ksq[:, c % 2, :], kb[:, c, :], kb[:, c, :],
                            [R(tg, "kb", c)], [R(tg, "ksq", c % 2)], eng="dve")
                        for st in range(NST):
                            MM(ps_c[:, 16 * st + 2 * c:16 * st + 2 * c + 2],
                               ksq[:, c % 2, 128 * st:128 * (st + 1)], g_ind2[:, :],
                               [R(tg, "ksq", c % 2), R("ind2")], [R(tg, "psc")],
                               True, True, tag=tg + f"dg{t}{c}{st}")
                    for st in range(NST):
                        ACT(ndiag[:, st, :], ps_c[:, 16 * st:16 * (st + 1)], AF.Identity,
                            [R(tg, "psc")], [R(tg, "ndiag", st)], scale=-1.0,
                            tag=tg + f"nd{t}{st}")
                        ACT(edia[:, st, :], ps_c[:, 16 * st:16 * (st + 1)], AF.Exp,
                            [R(tg, "psc")], [R(tg, "edia", st)], tag=tg + f"ed{t}{st}")
                    for half in range(2):
                        for st in range(NST):
                            MM(ps_d[half][0:1, :], g_onesbc[:, :],
                               vtm[:, st, 8 * half:8 * (half + 1), 0:64],
                               [R(tg, "vtm", st, half), R("onesbc")], [R(tg, "psd", half)],
                               st == 0, st == NST - 1, tag=tg + f"vs{t}{st}{half}")
                        ETT(ALU.add, vsum[:, 512 * half:512 * (half + 1)],
                            vsum[:, 512 * half:512 * (half + 1)], ps_d[half][0:1, :],
                            [R(tg, "psd", half), R(tg, "vsum")], [R(tg, "vsum")],
                            tag=tg + f"vsa{t}{half}")
                    for hp in range(NC8):
                        pb, pbreg = ps_b[hp % 2], R(tg, "psb", hp % 2)
                        for st in range(NST):
                            pa, preg = ps_a[st % 2], R(tg, "psa", st % 2)
                            MM(pa[:, :], kb[:, hp, 128 * st:128 * (st + 1)], g_projblk[:, :],
                               [R(tg, "kb", hp), R("projblk")], [preg],
                               True, True, tag=tg + f"D{t}{hp}{st}")
                            for half in range(2):
                                h = 2 * hp + half
                                ACT(ek[:, st, M * half:M * (half + 1)], pa[:, M * half:M * (half + 1)],
                                    AF.Exp, [preg, R(tg, "ndiag", st)], [R(tg, "ek", st, half)],
                                    bias=ndiag[:, st, h:h + 1], tag=tg + f"E{t}{hp}{st}{half}")
                                TS(mscr[:, :], ek[:, st, M * half:M * (half + 1)], 1.0, None,
                                   ALU.mult, ALU.max,
                                   [R(tg, "ek", st, half)], [R(tg, "mscr"), R(tg, "rmx", st, h)],
                                   accum=rmx[:, st, h:h + 1], tag=tg + f"RX{t}{hp}{st}{half}")
                        for half in range(2):
                            h = 2 * hp + half
                            for st in range(NST):
                                MM(pb[0:65, M * half:M * (half + 1)],
                                   vtm[:, st, h, 0:65], ek[:, st, M * half:M * (half + 1)],
                                   [R(tg, "vtm", st, h // 8), R(tg, "vtm1"),
                                    R(tg, "ek", st, half)],
                                   [pbreg], st == 0, st == NST - 1, tag=tg + f"C{t}{hp}{st}{half}")
                        if t == 0:
                            CPY(ctxs[:, hp, :], pb[:, :], [pbreg], [R(tg, "ctxs", hp)],
                                tag=tg + f"CA{t}{hp}")
                        else:
                            ETT(ALU.add, ctxs[:, hp, :], ctxs[:, hp, :], pb[:, :],
                                [pbreg, R(tg, "ctxs", hp)], [R(tg, "ctxs", hp)],
                                tag=tg + f"CA{t}{hp}")
                        if hp == 1 and t + 1 < NT:
                            a_stats(t + 1)
                        if hp == 3 and t + 1 < NT:
                            a_apply(t + 1)
                    for st in range(NST):
                        ETT(ALU.mult, rmx[:, st, :], rmx[:, st, :], edia[:, st, :],
                            [R(tg, "edia", st)] + [R(tg, "rmx", st, h) for h in range(H)],
                            [R(tg, "rmx", st, h) for h in range(H)], tag=tg + f"RM{t}{st}")
                        ETT(ALU.max, mrun[:, :], mrun[:, :], rmx[:, st, :],
                            [R(tg, "mrun")] + [R(tg, "rmx", st, h) for h in range(H)],
                            [R(tg, "mrun")], tag=tg + f"MR{t}{st}")

                RED("max", tmpA[:, 0, 0:1], mrun[:, :], [R(tg, "mrun"), R(tg, "tmpA", 0)],
                    [R(tg, "mcol"), R(tg, "tmpA", 0)], tag=tg + "mred")
                CPY(mcolb[:, :], tmpA[:, 0, 0:1], [R(tg, "mcol")], [R(tg, "mcolb")])
                TRANS(ps_e[0:1, 0:128], mcolb[:, :], [R(tg, "mcolb")], [R(tg, "pse")],
                      tag=tg + "mtr")
                RED("max", scs[:, 0:1], ps_e[0:1, 0:128], [R(tg, "pse")], [R(tg, "emax")],
                    tag=tg + "mred2")
                p.dma("sp", d_emin, scs[:, 0:1], [R(tg, "emax")], [R(tg, "emin")],
                      tag=tg + "em_s")
                for c in range(NC8):
                    p.dma("sp", d_ctxin[65 * c:65 * (c + 1), :], ctxs[:, c, :],
                          [R(tg, "ctxs", c)], [R(tg, "ctxin", c)], tag=tg + f"cx_s{c}")
                p.dma("sp", d_vsin, vsum[:, :], [R(tg, "vsum")], [R(tg, "vsin")],
                      tag=tg + "vs_s")
                p.emit("pool", lambda e: e.collective_compute(
                    "AllReduce", ALU.max, replica_groups=ALL8, ins=[d_emin], outs=[d_emout]),
                    [R(tg, "emin")], [R(tg, "emout")], cc=True, tag=tg + "em_ar")
                p.emit("pool", lambda e: e.collective_compute(
                    "AllReduce", ALU.add, replica_groups=PAIRS,
                    ins=[d_ctxin[0:4 * 65, :]], outs=[d_ctxout[0:4 * 65, :]]),
                    [R(tg, "ctxin", c) for c in range(4)],
                    [R(tg, "ctxoutA")], cc=True, tag=tg + "cx_arA")
                p.emit("pool", lambda e: e.collective_compute(
                    "AllReduce", ALU.add, replica_groups=PAIRS,
                    ins=[d_ctxin[4 * 65:8 * 65, :]], outs=[d_ctxout[4 * 65:8 * 65, :]]),
                    [R(tg, "ctxin", c) for c in range(4, NC8)],
                    [R(tg, "ctxoutB")], cc=True, tag=tg + "cx_arB")
                p.emit("pool", lambda e: e.collective_compute(
                    "AllReduce", ALU.add, replica_groups=PAIRS, ins=[d_vsin], outs=[d_vsout]),
                    [R(tg, "vsin")], [R(tg, "vsout")], cc=True, tag=tg + "vs_ar")

                for t in range(NT):
                    for c in range(NC8):
                        ETT(ALU.mult, ksq[:, c % 2, :], qh[:, c, TT * t:TT * (t + 1)],
                            qh[:, c, TT * t:TT * (t + 1)],
                            [R(tg, "qh", c, t)], [R(tg, "ksq", c % 2)], eng="dve")
                        for st in range(NST):
                            MM(ps_c[:, 16 * st + 2 * c:16 * st + 2 * c + 2],
                               ksq[:, c % 2, 128 * st:128 * (st + 1)], g_ind2[:, :],
                               [R(tg, "ksq", c % 2), R("ind2")], [R(tg, "psc")],
                               True, True, tag=tg + f"qdg{t}{c}{st}")
                    for hp in range(NC8):
                        for st in range(NST):
                            pa, preg = ps_a[st % 2], R(tg, "psa", st % 2)
                            MM(pa[:, :], qh[:, hp, TT * t + 128 * st:TT * t + 128 * (st + 1)],
                               g_projblk[:, :], [R(tg, "qh", hp, t), R("projblk")], [preg],
                               True, True, tag=tg + f"qD{t}{hp}{st}")
                            for half in range(2):
                                h = 2 * hp + half
                                TS(mscr[:, :], pa[:, M * half:M * (half + 1)], 1.0, None,
                                   ALU.mult, ALU.max,
                                   [preg], [R(tg, "mscr"), R(tg, "rmx", st, h)],
                                   accum=rmx[:, st, h:h + 1], tag=tg + f"qRX{t}{hp}{st}{half}")
                    for st in range(NST):
                        ETT(ALU.add, gin_[:, :], ps_c[:, 16 * st:16 * (st + 1)], rmx[:, st, :],
                            [R(tg, "psc")] + [R(tg, "rmx", st, h) for h in range(H)],
                            [R(tg, "gin")], tag=tg + f"qg{t}{st}")
                        ACT(gin_[:, :], gin_[:, :], AF.Exp, [R(tg, "gin"), R("negUle")],
                            [R(tg, "gin")], bias=g_negUle[:, :], tag=tg + f"qge{t}{st}")
                        CPY(sq128[:, :], gin_[:, :], [R(tg, "gin")], [R(tg, "sq128")])
                        TRANS(ps_e[0:16, 0:128], sq128[:, :], [R(tg, "sq128")], [R(tg, "pse")],
                              tag=tg + f"qgt{t}{st}")
                        CPY(gTb[:, TT * t + 128 * st:TT * t + 128 * (st + 1)],
                            ps_e[0:16, 0:128], [R(tg, "pse")], [R(tg, "gTb", t)],
                            tag=tg + f"qgc{t}{st}")
            p.barrier()

            with ExitStack() as fb:
                FB = fb.enter_context
                wo = FB(nc.sbuf_tensor(tg + "wo", [128, NC8, D], BF16))
                vse = FB(nc.sbuf_tensor(tg + "vse", [65, H], F32))
                cxb = FB(nc.sbuf_tensor(tg + "cxb", [65, M], BF16))
                ctxT = FB(nc.sbuf_tensor(tg + "ctxT", [128, H * 2 * 65], BF16))
                csM16 = FB(nc.sbuf_tensor(tg + "csM16", [16, H * 65], BF16))
                csmsk = FB(nc.sbuf_tensor(tg + "csmsk", [65, 16], BF16))
                csf = FB(nc.sbuf_tensor(tg + "csf", [65, 1], F32))
                csb = FB(nc.sbuf_tensor(tg + "csb", [65, 1], BF16))
                attn = FB(nc.sbuf_tensor(tg + "attn", [128, 2, NC8, TT], BF16))
                dinvs = FB(nc.sbuf_tensor(tg + "dinvs", [1, 2, TT], F32))
                num = FB(nc.sbuf_tensor(tg + "num", [64, 3, TT], F32))
                dinbs = FB(nc.sbuf_tensor(tg + "dinbs", [64, 3, TT], F32))
                eq = FB(nc.sbuf_tensor(tg + "eq", [128, 2, 2, TT], BF16))
                einv = FB(nc.sbuf_tensor(tg + "einv", [65, 1], F32))
                fsb = FB(nc.sbuf_tensor(tg + "fsb", [128, NC8, TT], BF16))

                for c in range(NC8):
                    p.dma("pool", wo[:, c, :], dw["wo"][128 * c:128 * (c + 1), :],
                          [], [R(tg, "wo", c)], tag=tg + f"wo{c}")

                p.dma("sp", scs[:, 1:2], d_emout, [R(tg, "emout")], [R(tg, "emr")],
                      tag=tg + "em_b")
                p.dve(lambda e: e.reciprocal(scs[:, 2:3], scs[:, 1:2]),
                      [R(tg, "emr")], [R(tg, "einvs")], tag=tg + "einv")
                MM(ps_a[0][0:65, 0:1], g_onesfr[0:1, 0:65], scs[0:1, 2:3],
                   [R(tg, "einvs"), R("onesfr")], [R(tg, "psa", 0)], True, True,
                   tag=tg + "einvbc")
                CPY(einv[:, :], ps_a[0][0:65, 0:1], [R(tg, "psa", 0)], [R(tg, "einvc")])
                p.dma("sp", vsum[:, :], d_vsout, [R(tg, "vsout")], [R(tg, "vsum")],
                      tag=tg + "vs_b")
                for h in range(H):
                    MM(ps_a[0][0:64, 0:1], vsum[0:1, DH * h:DH * (h + 1)], g_onesf1[:, :],
                       [R(tg, "vsum"), R("onesf1")], [R(tg, "psa", 0)], True, True,
                       tag=tg + f"vseT{h}")
                    TS(vse[0:64, h:h + 1], ps_a[0][0:64, 0:1], KEPS, None, ALU.mult,
                       ALU.bypass, [R(tg, "psa", 0)], [R(tg, "vse", h)], tag=tg + f"vse{h}")
                    MSET(vse[64:65, h:h + 1], KEPS * S, [R(tg, "vse", h)], [R(tg, "vse", h)])
                for c in range(NC8):
                    p.dma("sp", ctxs[:, c, :], d_ctxout[65 * c:65 * (c + 1), :],
                          [R(tg, "ctxoutA" if c < 4 else "ctxoutB")], [R(tg, "ctxs", c)],
                          tag=tg + f"cx_b{c}")
                for h in range(H):
                    hp, half = h // 2, h % 2
                    TS(cxb[:, :], ctxs[:, hp, M * half:M * (half + 1)],
                       einv[:, 0:1], vse[:, h:h + 1], ALU.mult, ALU.add,
                       [R(tg, "ctxs", hp), R(tg, "einvc"), R(tg, "vse", h)], [R(tg, "cxb")],
                       tag=tg + f"cfix{h}")
                    RED("sum", csf[:, :], cxb[:, :], [R(tg, "cxb")], [R(tg, "csf")],
                        tag=tg + f"csum{h}")
                    CPY(csb[:, :], csf[:, :], [R(tg, "csf")], [R(tg, "csb")])
                    MSET(csmsk[:, :], 0.0, [R(tg, "csmsk")], [R(tg, "csmsk")])
                    CPY(csmsk[:, h:h + 1], csb[:, :], [R(tg, "csb"), R(tg, "csmsk")],
                        [R(tg, "csmsk")])
                    TRANS(ps_e[0:16, 0:65], csmsk[:, :], [R(tg, "csmsk")], [R(tg, "pse")],
                          tag=tg + f"csT{h}", pdim=65)
                    CPY(csM16[:, 65 * h:65 * (h + 1)], ps_e[0:16, 0:65], [R(tg, "pse")],
                        [R(tg, "csT", h)])
                    for mc in range(2):
                        TRANS(ps_e[0:128, 0:65], cxb[:, 128 * mc:128 * (mc + 1)],
                              [R(tg, "cxb")], [R(tg, "pse")], tag=tg + f"cT{h}{mc}", pdim=65)
                        CPY(ctxT[:, (2 * h + mc) * 65:(2 * h + mc + 1) * 65],
                            ps_e[0:128, 0:65], [R(tg, "pse")], [R(tg, "ctxT", h, mc)])

                def x2M(t, h):
                    hp, half = h // 2, h % 2
                    bse = 64 * half
                    ETT(ALU.mult, attn[bse:bse + 64, t % 2, hp, :], num[:, h % 3, :],
                        dinbs[:, h % 3, :],
                        [R(tg, "num", h % 3), R(tg, "dinb", h % 3)],
                        [R(tg, "attn", t % 2, hp, half)],
                        tag=tg + f"x2M{t}{h}")

                def x2D(t, h):
                    hp, half = h // 2, h % 2
                    banks = ps_a if h % 2 == 0 else ps_d
                    bnm = "psa" if h % 2 == 0 else "psd"
                    for mc in range(2):
                        pa, preg = banks[mc], R(tg, bnm, mc)
                        MM(pa[:, :], g_projT2[64 * half:64 * half + 64, 128 * mc:128 * (mc + 1)],
                           qh[64 * half:64 * half + 64, hp, TT * t:TT * (t + 1)],
                           [R("projT2"), R(tg, "qh", hp, t)], [preg],
                           True, True, tag=tg + f"x2D{t}{h}{mc}")
                        ACT(eq[:, h % 2, mc, :], pa[:, :], AF.Exp,
                            [preg, R("negU")], [R(tg, "eq", h % 2, mc)],
                            bias=g_negU[:, :], tag=tg + f"x2E{t}{h}{mc}")

                def x2AG(t, h):
                    pb, pbreg = ps_b[h % 2], R(tg, "psb", h % 2)
                    for mc in range(2):
                        MM(pb[:, :], ctxT[:, (2 * h + mc) * 65:(2 * h + mc + 1) * 65],
                           eq[:, h % 2, mc, :],
                           [R(tg, "ctxT", h, mc), R(tg, "eq", h % 2, mc)], [pbreg],
                           mc == 0, False, tag=tg + f"x2A{t}{h}{mc}")
                    MM(pb[:, :], csM16[:, 65 * h:65 * (h + 1)],
                       gTb[0:16, TT * t:TT * (t + 1)],
                       [R(tg, "csT", h), R(tg, "gTb", t)], [pbreg],
                       False, True, tag=tg + f"x2G{t}{h}")
                    CPY(num[:, h % 3, :], pb[0:64, :], [pbreg], [R(tg, "num", h % 3)],
                        tag=tg + f"x2N{t}{h}")
                    ACT(dinvs[:, h % 2, :], pb[64:65, :], AF.Ln,
                        [pbreg], [R(tg, "dinv", h % 2)], tag=tg + f"x2R{t}{h}")
                    ACT(dinvs[:, h % 2, :], dinvs[:, h % 2, :], AF.Exp,
                        [R(tg, "dinv", h % 2)], [R(tg, "dinv", h % 2)],
                        scale=-1.0, tag=tg + f"x2R2{t}{h}")
                    BCAST(dinvs[:, h % 2, :], [R(tg, "dinv", h % 2)],
                          dinbs[:, h % 3, :], R(tg, "dinb", h % 3), 64,
                          tag=tg + f"x2B{t}{h}")

                def O_chunk(tt, m):
                    p.dma("sp", xt[:, m, :], d_x1[128 * m:128 * (m + 1), TT * tt:TT * (tt + 1)],
                          [R("dx1", m, tt)], [R(tg, "xt", m)], tag=tg + f"lq{tt}_{m}")
                    for k in range(NC8):
                        MM(ps_c[:, :], wo[:, k, 128 * m:128 * (m + 1)], attn[:, tt % 2, k, :],
                           [R(tg, "wo", k), R(tg, "attn", tt % 2, k, 0), R(tg, "attn", tt % 2, k, 1)],
                           [R(tg, "psc")], k == 0, k == NC8 - 1, tag=tg + f"O{tt}{m}{k}")
                    ACT(tmpA[:, m % 2, :], ps_c[:, :], AF.Identity, [R(tg, "psc"), R("g_bo")],
                        [R(tg, "tmpA", m % 2)], bias=g_bo[:, m:m + 1], tag=tg + f"Oe{tt}{m}")
                    ETT(ALU.add, tmpA[:, m % 2, :], tmpA[:, m % 2, :], xt[:, m, :],
                        [R(tg, "tmpA", m % 2), R(tg, "xt", m)], [R(tg, "tmpA", m % 2)],
                        eng="pool", tag=tg + f"Oa{tt}{m}")
                    p.dma("sp", d_x2[128 * m:128 * (m + 1), TT * tt:TT * (tt + 1)],
                          tmpA[:, m % 2, :], [R(tg, "tmpA", m % 2)], [R("dx2", m, tt)],
                          tag=tg + f"Os{tt}{m}")
                    CPY(fsb[:, m, :], tmpA[:, m % 2, :],
                        [R(tg, "tmpA", m % 2)], [R(tg, "fsb", m)], tag=tg + f"fsc{tt}{m}")

                def O_stats(tt):
                    for m in range(NC8):
                        cst = fsb[:, m, :]
                        MM(ps_d[0][0:1, 0:TT], g_onesbc[:, :], cst,
                           [R(tg, "fsb", m), R("onesbc")], [R(tg, "psd", 0)],
                           m == 0, m == NC8 - 1, tag=tg + f"fsm{tt}{m}")
                        ETT(ALU.mult, cst, cst, cst, [R(tg, "fsb", m)], [R(tg, "fsb", m)])
                        MM(ps_d[1][0:1, 0:TT], g_onesbc[:, :], cst,
                           [R(tg, "fsb", m), R("onesbc")], [R(tg, "psd", 1)],
                           m == 0, m == NC8 - 1, tag=tg + f"fsq{tt}{m}")
                    mean, var = dinvs[:, 0, :], dinvs[:, 1, :]
                    ACT(mean, ps_d[0][0:1, 0:TT], AF.Identity,
                        [R(tg, "psd", 0)], [R(tg, "dinv", 0)], scale=1.0 / D)
                    ETT(ALU.mult, var, mean, mean, [R(tg, "dinv", 0)], [R(tg, "dinv", 1)])
                    STT(var, ps_d[1][0:1, 0:TT], 1.0 / D, var, ALU.mult, ALU.subtract,
                        [R(tg, "psd", 1), R(tg, "dinv", 1)], [R(tg, "dinv", 1)])
                    ACT(var, var, AF.Ln, [R(tg, "dinv", 1), R("epsln")],
                        [R(tg, "dinv", 1)], bias=g_epsln[:, :])
                    ACT(var, var, AF.Exp, [R(tg, "dinv", 1)], [R(tg, "dinv", 1)], scale=-0.5)
                    STT(mean, mean, -1.0, var, ALU.mult, ALU.mult,
                        [R(tg, "dinv", 0), R(tg, "dinv", 1)], [R(tg, "dinv", 0)])
                    p.dma("sp", d_bch[16 + 2 * tt:17 + 2 * tt, :], var,
                          [R(tg, "dinv", 1)], [R("bch", 16 + 2 * tt)], tag=tg + f"fso{tt}")
                    p.dma("sp", d_bch[17 + 2 * tt:18 + 2 * tt, :], mean,
                          [R(tg, "dinv", 0)], [R("bch", 17 + 2 * tt)], tag=tg + f"fsn{tt}")

                for t in range(NT):
                    x2D(t, 0)
                    for h in range(H):
                        if h + 1 < H:
                            x2D(t, h + 1)
                        x2AG(t, h)
                        if h >= 2:
                            x2M(t, h - 2)
                        if t >= 1 and h % 2 == 1:
                            O_chunk(t - 1, h // 2)
                    x2M(t, H - 2)
                    x2M(t, H - 1)
                    if t >= 1:
                        O_stats(t - 1)
                for m in range(NC8):
                    O_chunk(NT - 1, m)
                O_stats(NT - 1)


    emit_ffn(1, d_xT, d_x1, "xTin", "dx1")
    p.barrier()
    emit_attn()
    p.barrier()
    emit_ffn(2, d_x2, d_out, "dx2", "dout")
    counts = p.build()
    gstack.close()
    return nc, counts


_NC_CACHE = None


def _get_nc():
    global _NC_CACHE
    if _NC_CACHE is None:
        _NC_CACHE = build_nc()
    return _NC_CACHE


def kernel(**inputs):
    nc, _ = _get_nc()
    x = np.asarray(inputs["L0_attn_x"], dtype=np.float32)
    row = lambda a: np.ascontiguousarray(np.asarray(a, np.float32).reshape(1, -1))
    base = {
        "wq": np.ascontiguousarray(inputs["L0_attn_wq"], np.float32),
        "wk": np.ascontiguousarray(inputs["L0_attn_wk"], np.float32),
        "wv": np.ascontiguousarray(inputs["L0_attn_wv"], np.float32),
        "wo": np.ascontiguousarray(inputs["L0_attn_wo"], np.float32),
        "proj": np.ascontiguousarray(inputs["L0_attn_proj"], np.float32),
        "cn_w1": np.ascontiguousarray(inputs["L0_ffn_cn_w1"], np.float32),
        "cn_w2": np.ascontiguousarray(inputs["L0_ffn_cn_w2"], np.float32),
        "cn_w3": np.ascontiguousarray(inputs["L0_ffn_cn_w3"], np.float32),
        "gc_w1": np.ascontiguousarray(inputs["L0_ffn_gc_w1"], np.float32),
        "gc_w2": np.ascontiguousarray(inputs["L0_ffn_gc_w2"], np.float32),
        "bo": row(inputs["L0_attn_bo"]),
        "cn_b1": row(inputs["L0_ffn_cn_b1"]),
        "cn_b2": row(inputs["L0_ffn_cn_b2"]),
        "cn_b3": row(inputs["L0_ffn_cn_b3"]),
        "gc_b1": row(inputs["L0_ffn_gc_b1"]),
        "gc_b2": row(inputs["L0_ffn_gc_b2"]),
        "cn_w4r": row(inputs["L0_ffn_cn_w4"]),
        "gc_w3r": row(inputs["L0_ffn_gc_w3"]),
        "cn_b4": row(inputs["L0_ffn_cn_b4"]).reshape(1, 1),
        "gc_b3": row(inputs["L0_ffn_gc_b3"]).reshape(1, 1),
        "arange": row(inputs["L0_ffn_adaptive_range"]).reshape(1, 1),
    }
    in_maps = []
    for core in range(8):
        bidx, half = core // 2, core % 2
        xs = x[bidx, TOK * half:TOK * (half + 1), :]
        m = dict(base)
        m["xT"] = np.ascontiguousarray(xs.T)
        in_maps.append(m)
    res = run_bass_kernel_spmd(nc, in_maps, core_ids=list(range(8)))
    out = np.empty((B, S, D), dtype=np.float32)
    for core in range(8):
        bidx, half = core // 2, core % 2
        out[bidx, TOK * half:TOK * (half + 1), :] = np.asarray(res.results[core]["oT"]).T
    return out


if __name__ == "__main__":
    nc, counts = build_nc()
    try:
        n = nc.n_instructions()
    except Exception:
        n = -1
    print("built;", n, "instructions")
    print("counts:", counts)


# version 55
# speedup vs baseline: 1.0129x; 1.0129x over previous
import sys
sys.path.insert(0, "/opt/trn_rl_repo")
import math
from contextlib import ExitStack

import numpy as np
import concourse.bass as bass
import concourse.mybir as mybir
from concourse.bass_utils import run_bass_kernel_spmd

F32 = mybir.dt.float32
BF16 = mybir.dt.bfloat16
AF = mybir.ActivationFunctionType
ALU = mybir.AluOpType
AXX = mybir.AxisListType.X

ENGINES = ("pe", "act", "dve", "pool", "sp")
NDQ = 6
NCC = 2


class _Op:
    __slots__ = ("idx", "engine", "fn", "reads", "writes", "kind", "tag",
                 "chan", "inc_amt", "needs_inc", "count_after", "waits",
                 "clock", "_deps", "pre_wait")

    def __init__(self, idx, engine, fn, reads, writes, kind, tag):
        self.idx = idx
        self.engine = engine
        self.fn = fn
        self.reads = tuple(reads)
        self.writes = tuple(writes)
        self.kind = kind
        self.tag = tag
        self.chan = engine
        self.inc_amt = 1
        self.needs_inc = kind != "compute"
        self.count_after = None
        self.waits = None
        self.clock = None
        self.pre_wait = None


class Prog:

    def __init__(self, nc):
        self.nc = nc
        self.ops = []
        self.last_writer = {}
        self.readers = {}
        self.dq_counts = {}
        self.cc_count = 0
        self.chan_last = {}
        self.stats = {"waits": 0, "incs": 0}

    def emit(self, engine, fn, reads=(), writes=(), dma=False, cc=False,
             tag="", after=()):
        assert engine in ENGINES, engine
        kind = "cc" if cc else ("dma" if dma else "compute")
        op = _Op(len(self.ops), engine, fn, reads, writes, kind, tag)
        if kind == "dma":
            j = self.dq_counts.get(engine, 0)
            self.dq_counts[engine] = j + 1
            op.chan = f"dq_{engine}_{j % NDQ}"
            op.inc_amt = 16
            op.count_after = 16 * (j // NDQ + 1)
            if j >= NDQ:
                op.pre_wait = (op.chan, 16 * (j // NDQ))
        elif kind == "cc":
            j = self.cc_count
            self.cc_count = j + 1
            op.chan = f"cc_{j % NCC}"
            op.inc_amt = 1
            op.count_after = j // NCC + 1
            if j >= NCC:
                op.pre_wait = (op.chan, j // NCC)
        self.ops.append(op)
        deps = list(after)
        for r in op.reads:
            w = self.last_writer.get(r)
            if w is not None:
                deps.append(w)
        for r in op.writes:
            w = self.last_writer.get(r)
            if w is not None:
                deps.append(w)
            for rd in self.readers.get(r, ()):
                deps.append(rd)
        op._deps = [d for d in dict.fromkeys(deps)
                    if not (d.engine == "pe" and op.engine == "pe"
                            and d.kind == "compute" and op.kind == "compute")
                    and d is not op]
        for d in op._deps:
            if d.kind == "compute":
                d.needs_inc = True
        for r in op.reads:
            self.readers.setdefault(r, []).append(op)
        for r in op.writes:
            self.last_writer[r] = op
            self.readers[r] = []
        self.chan_last[op.chan if kind != "compute" else engine] = op
        return op

    def pe(self, fn, reads=(), writes=(), tag=""):
        return self.emit("pe", fn, reads, writes, tag=tag)

    def act(self, fn, reads=(), writes=(), tag=""):
        return self.emit("act", fn, reads, writes, tag=tag)

    def dve(self, fn, reads=(), writes=(), tag=""):
        return self.emit("dve", fn, reads, writes, tag=tag)

    def pool(self, fn, reads=(), writes=(), tag=""):
        return self.emit("pool", fn, reads, writes, tag=tag)

    def dma(self, engine, out, in_, reads=(), writes=(), tag="", **kw):
        return self.emit(engine,
                         lambda e, o=out, i=in_, kw=kw: e.dma_start(out=o, in_=i, **kw),
                         reads, writes, dma=True, tag=tag)

    def barrier(self):
        pending = list(dict.fromkeys(self.chan_last.values()))
        for e in ENGINES:
            self.emit(e, lambda eng: eng.nop(), tag="barrier", after=pending)

    def build(self, extra_guard=True):
        nc = self.nc
        counts = {}
        for op in self.ops:
            if op.kind == "compute":
                if op.needs_inc:
                    counts[op.chan] = counts.get(op.chan, 0) + 1
                    op.count_after = counts[op.chan]
            else:
                counts[op.chan] = max(counts.get(op.chan, 0), op.count_after)
        clocks = {e: {} for e in ENGINES}
        for op in self.ops:
            clk = clocks[op.engine]
            if op.pre_wait is not None:
                c, v = op.pre_wait
                if clk.get(c, 0) >= v:
                    op.pre_wait = None
                else:
                    clk[c] = v
            waits = {}
            for d in op._deps:
                if not d.needs_inc:
                    continue
                tgt = d.count_after
                if clk.get(d.chan, 0) >= tgt:
                    continue
                waits[d.chan] = max(waits.get(d.chan, 0), tgt)
            for chan, tgt in waits.items():
                clk[chan] = max(clk.get(chan, 0), tgt)
            for d in op._deps:
                if d.needs_inc and d.clock is not None and clk.get(d.chan, 0) >= d.count_after:
                    for c2, v2 in d.clock.items():
                        if v2 > clk.get(c2, 0):
                            clk[c2] = v2
            op.waits = waits
            snap = dict(clk)
            snap[op.chan] = max(snap.get(op.chan, 0), op.count_after or 0)
            op.clock = snap
            self.stats["waits"] += len(waits) + (1 if op.pre_wait else 0)
            self.stats["incs"] += 1 if op.needs_inc else 0

        final_counts = dict(counts)
        sems = {}
        stack = ExitStack()
        for c in sorted(final_counts):
            if final_counts[c] > 0:
                sems[c] = stack.enter_context(nc.semaphore("sem_" + c))
        block = stack.enter_context(nc.Block())
        by_engine = {e: [op for op in self.ops if op.engine == e] for e in ENGINES}
        eng_attr = {"pe": "tensor", "act": "scalar", "dve": "vector",
                    "pool": "gpsimd", "sp": "sync"}

        def emit_engine(e, ops):
            def body(eng):
                for op in ops:
                    if op.pre_wait is not None:
                        eng.wait_ge(sems[op.pre_wait[0]], op.pre_wait[1])
                    for chan, tgt in sorted(op.waits.items()):
                        eng.wait_ge(sems[chan], tgt)
                    inst = op.fn(eng)
                    if op.needs_inc:
                        if inst is None:
                            raise RuntimeError(f"op {op.tag or op.idx} needs inc, fn None")
                        inst.then_inc(sems[op.chan], op.inc_amt)
                if extra_guard:
                    for chan, sem in sems.items():
                        eng.wait_ge(sem, final_counts[chan])
            return body

        for e in ENGINES:
            getattr(block, eng_attr[e])(emit_engine(e, by_engine[e]))
        stack.close()
        return final_counts


B, S, D = 4, 4096, 1024
H, DH, M = 16, 64, 256
TOK = 2048
NT, TT, NST, NC8 = 4, 512, 4, 8
DN = float(DH) ** -0.25
KEPS = 1e-4
LNEPS = 1e-5
NBCH = 24
PAIRS = [[0, 1], [2, 3], [4, 5], [6, 7]]
ALL8 = [[0, 1, 2, 3, 4, 5, 6, 7]]
TW_PERM = [4, 5, 6, 7, 0, 1, 2, 3]
TW_SIGN = [1.0, 1.0, -1.0, -1.0, -1.0, -1.0, 1.0, 1.0]


def build_nc():
    nc = bass.Bass("TRN2", target_bir_lowering=False, debug=False)

    d_xT = nc.dram_tensor("xT", [D, TOK], F32, kind="ExternalInput").ap()
    d_out = nc.dram_tensor("oT", [D, TOK], F32, kind="ExternalOutput").ap()
    d_x1 = nc.dram_tensor("x1s", [D, TOK], F32, kind="Internal").ap()
    d_x2 = nc.dram_tensor("x2s", [D, TOK], F32, kind="Internal").ap()
    d_bch = nc.dram_tensor("bch", [NBCH, TT], F32, kind="Internal").ap()
    dw = {}
    for nm, shp in [("wq", [D, D]), ("wk", [D, D]), ("wv", [D, D]), ("wo", [D, D]),
                    ("proj", [M, DH]),
                    ("cn_w1", [D, D]), ("cn_w2", [D, 512]), ("cn_w3", [512, 256]),
                    ("gc_w1", [D, 512]), ("gc_w2", [512, 256])]:
        dw[nm] = nc.dram_tensor(nm, shp, F32, kind="ExternalInput").ap()
    for nm, n in [("bo", D), ("cn_b1", D), ("cn_b2", 512), ("cn_b3", 256),
                  ("gc_b1", 512), ("gc_b2", 256), ("cn_w4r", 256), ("gc_w3r", 256)]:
        dw[nm] = nc.dram_tensor(nm, [1, n], F32, kind="ExternalInput").ap()
    for nm in ("cn_b4", "gc_b3", "arange"):
        dw[nm] = nc.dram_tensor(nm, [1, 1], F32, kind="ExternalInput").ap()
    d_gin = [nc.dram_tensor(f"gin{i}", [128, NC8], F32, kind="Internal").ap() for i in range(2)]
    d_gout = [nc.dram_tensor(f"gout{i}", [128, NC8], F32, kind="Internal").ap() for i in range(2)]
    d_ctxin = nc.dram_tensor("ctxin", [8 * 65, TT], F32, kind="Internal").ap()
    d_ctxout = nc.dram_tensor("ctxout", [8 * 65, TT], F32, kind="Internal").ap()
    d_vsin = nc.dram_tensor("vsin", [1, D], F32, kind="Internal").ap()
    d_vsout = nc.dram_tensor("vsout", [1, D], F32, kind="Internal").ap()
    d_emin = nc.dram_tensor("emin", [1, 1], F32, kind="Internal").ap()
    d_emout = nc.dram_tensor("emout", [1, 1], F32, kind="Internal").ap()

    p = Prog(nc)
    gstack = ExitStack()
    GE = gstack.enter_context
    R = lambda *k: k

    g_ident = GE(nc.sbuf_tensor("ident", [128, 128], BF16))
    g_onesbr = GE(nc.sbuf_tensor("onesbr", [1, 128], BF16))
    g_onesfr = GE(nc.sbuf_tensor("onesfr", [1, 128], F32))
    g_onesbc = GE(nc.sbuf_tensor("onesbc", [128, 1], BF16))
    g_onesf1 = GE(nc.sbuf_tensor("onesf1", [1, 1], F32))
    g_ind2 = GE(nc.sbuf_tensor("ind2", [128, 2], BF16))
    g_projT2 = GE(nc.sbuf_tensor("projT2", [128, M], BF16))
    g_projblk = GE(nc.sbuf_tensor("projblk", [128, 2 * M], BF16))
    g_negU = GE(nc.sbuf_tensor("negU", [128, 1], F32))
    g_negUle = GE(nc.sbuf_tensor("negUle", [128, 1], F32))
    g_epsln = GE(nc.sbuf_tensor("epsln", [1, 1], F32))
    g_bo = GE(nc.sbuf_tensor("g_bo", [128, NC8], F32))
    g_b1 = GE(nc.sbuf_tensor("g_b1", [128, NC8], F32))
    g_b2 = GE(nc.sbuf_tensor("g_b2", [128, 4], F32))
    g_b3 = GE(nc.sbuf_tensor("g_b3", [128, 2], F32))
    g_gb1 = GE(nc.sbuf_tensor("g_gb1", [128, 4], F32))
    g_gb2 = GE(nc.sbuf_tensor("g_gb2", [128, 2], F32))
    g_w4 = GE(nc.sbuf_tensor("g_w4", [128, 2], BF16))
    g_gw3 = GE(nc.sbuf_tensor("g_gw3", [128, 2], BF16))
    g_halfb4 = GE(nc.sbuf_tensor("g_halfb4", [1, 1], F32))
    g_halfgb3 = GE(nc.sbuf_tensor("g_halfgb3", [1, 1], F32))
    g_ar = GE(nc.sbuf_tensor("g_ar", [1, 1], F32))
    g_sc = GE(nc.sbuf_tensor("g_sc", [1, 8], F32))
    g_w3m = GE(nc.sbuf_tensor("g_w3m", [128, 4, 256], BF16))
    g_gw1 = GE(nc.sbuf_tensor("g_gw1", [128, NC8, 512], BF16))
    g_gw2 = GE(nc.sbuf_tensor("g_gw2", [128, 4, 256], BF16))
    g_w2m = GE(nc.sbuf_tensor("g_w2m", [128, NC8, 512], BF16))

    def MM(out, lhsT, rhs, reads, writes, start, stop, tag=""):
        p.pe(lambda e, o=out, l=lhsT, r=rhs, s=start, sp_=stop:
             e.matmul(o, l, r, start=s, stop=sp_), reads, writes, tag=tag)

    def TRANS(out, in_, reads, writes, tag="", pdim=128):
        p.pe(lambda e, o=out, i=in_, pd=pdim: e.transpose(o, i, g_ident[0:pd, 0:pd]),
             list(reads) + [R("ident")], writes, tag=tag)

    def ACT(out, in_, func, reads, writes, bias=None, scale=1.0, tag=""):
        if bias is None:
            p.act(lambda e, o=out, i=in_, f=func, s=scale:
                  e.activation(o, i, f, scale=s), reads, writes, tag=tag)
        else:
            p.act(lambda e, o=out, i=in_, f=func, s=scale, b=bias:
                  e.activation(o, i, f, bias=b, scale=s), reads, writes, tag=tag)

    def CPY(out, in_, reads, writes, eng="dve", tag=""):
        p.emit(eng, lambda e, o=out, i=in_: e.tensor_copy(o, i), reads, writes, tag=tag)

    def ETT(op, out, a, b, reads, writes, eng="dve", tag=""):
        p.emit(eng, lambda e, o=out, x=a, y=b, v=op: e.tensor_tensor(o, x, y, v),
               reads, writes, tag=tag)

    def TS(out, in_, s1, s2, op0, op1, reads, writes, accum=None, eng="dve", tag=""):
        p.emit(eng, lambda e, o=out, i=in_, a=s1, b=s2, o0=op0, o1=op1, ac=accum:
               e.tensor_scalar(o, i, a, b, op0=o0, op1=o1, accum_out=ac),
               reads, writes, tag=tag)

    def STT(out, in0, scl, in1, op0, op1, reads, writes, accum=None, eng="dve", tag=""):
        p.emit(eng, lambda e, o=out, i0=in0, s=scl, i1=in1, o0=op0, o1=op1, ac=accum:
               e.scalar_tensor_tensor(o, i0, s, i1, op0=o0, op1=o1, accum_out=ac),
               reads, writes, tag=tag)

    def RED(kind, out, in_, reads, writes, tag=""):
        if kind == "max":
            p.dve(lambda e, o=out, i=in_: e.reduce_max(o, i, axis=AXX), reads, writes, tag=tag)
        else:
            p.dve(lambda e, o=out, i=in_: e.reduce_sum(o, i, axis=AXX), reads, writes, tag=tag)

    def MSET(ap_, v, reads, writes, eng="dve", tag=""):
        p.emit(eng, lambda e, a=ap_, x=v: e.memset(a, x), reads, writes, tag=tag)

    bcslot = [0]

    def BCAST(row_ap, row_reads, dst_ap, dst_reg, npart, tag=""):
        s = bcslot[0] % 16
        bcslot[0] += 1
        p.dma("sp", d_bch[s:s + 1, :], row_ap, list(row_reads), [R("bch", s)],
              tag=tag + "o")
        src = d_bch[s:s + 1, 0:TT].squeeze(0).partition_broadcast(npart)
        p.dma("sp", dst_ap, src, [R("bch", s)], [dst_reg], tag=tag + "i")

    def LN_STATS(tg, xap, xreg, xb, pr0, pr0reg, pr1, pr1reg, stat, pi,
                 bc0ap, bc0reg, bc1ap, bc1reg, t, cast_eng="act"):
        for c in range(NC8):
            xbc = xb[:, c % 2, :]
            xbr = R(tg, "xb", c % 2)
            if cast_eng == "act":
                ACT(xbc, xap(c), AF.Copy, [xreg(c)], [xbr], tag=tg + f"cst{t}{c}")
            else:
                CPY(xbc, xap(c), [xreg(c)], [xbr], eng=cast_eng, tag=tg + f"cst{t}{c}")
            MM(pr0, g_onesbc[:, :], xbc, [xbr, R("onesbc")], [pr0reg],
               c == 0, c == NC8 - 1, tag=tg + f"sm{t}{c}")
            ETT(ALU.mult, xbc, xbc, xbc, [xbr], [xbr], tag=tg + f"sqm{t}{c}")
            MM(pr1, g_onesbc[:, :], xbc, [xbr, R("onesbc")], [pr1reg],
               c == 0, c == NC8 - 1, tag=tg + f"sq{t}{c}")
        mean, var = stat[:, 0:TT], stat[:, TT:2 * TT]
        ACT(mean, pr0, AF.Identity, [pr0reg], [R(tg, "st0", pi)], scale=1.0 / D)
        ETT(ALU.mult, var, mean, mean, [R(tg, "st0", pi)], [R(tg, "st1", pi)])
        STT(var, pr1, 1.0 / D, var, ALU.mult, ALU.subtract,
            [pr1reg, R(tg, "st1", pi)], [R(tg, "st1", pi)], tag=tg + f"var{t}")
        ACT(var, var, AF.Ln, [R(tg, "st1", pi), R("epsln")], [R(tg, "st1", pi)],
            bias=g_epsln[:, :])
        ACT(var, var, AF.Exp, [R(tg, "st1", pi)], [R(tg, "st1", pi)], scale=-0.5)
        STT(mean, mean, -1.0, var, ALU.mult, ALU.mult,
            [R(tg, "st0", pi), R(tg, "st1", pi)], [R(tg, "st0", pi)])
        BCAST(var, [R(tg, "st1", pi)], bc0ap, bc0reg, 128, tag=tg + f"b0_{t}")
        BCAST(mean, [R(tg, "st0", pi)], bc1ap, bc1reg, 128, tag=tg + f"b1_{t}")

    with nc.psum_tensor("pi_t", [128, 128], F32) as pi_t, \
         nc.psum_tensor("pi_b", [128, 128], BF16) as pi_b, \
         nc.sbuf_tensor("i_row", [1, 8, D], F32) as i_rows, \
         nc.sbuf_tensor("i_rowb", [1, 128], BF16) as i_rowb, \
         nc.sbuf_tensor("i_proj", [128, DH], F32) as i_proj, \
         nc.sbuf_tensor("i_projb", [128, DH], BF16) as i_projb, \
         nc.sbuf_tensor("i_ps", [128, DH], F32) as i_ps, \
         nc.sbuf_tensor("i_col", [128, 2], F32) as i_col, \
         nc.sbuf_tensor("i_colb", [128, 1], BF16) as i_colb:

        MSET(g_ident[:, :], 0.0, [], [R("ident")], eng="pool")
        p.pool(lambda e: e.affine_select(
            out=g_ident[:, :], in_=g_ident[:, :], compare_op=ALU.not_equal,
            fill=1.0, base=0, pattern=[[-1, 128]], channel_multiplier=1),
            [R("ident")], [R("ident")], tag="identsel")
        MSET(g_onesbr[:, :], 1.0, [], [R("onesbr")])
        MSET(g_onesfr[:, :], 1.0, [], [R("onesfr")])
        MSET(g_onesbc[:, :], 1.0, [], [R("onesbc")])
        MSET(g_onesf1[:, :], 1.0, [], [R("onesf1")])
        MSET(g_ind2[:, :], 0.0, [], [R("ind2")])
        MSET(g_ind2[0:64, 0:1], 0.5, [R("ind2")], [R("ind2")])
        MSET(g_ind2[64:128, 1:2], 0.5, [R("ind2")], [R("ind2")])
        MSET(g_epsln[:, :], LNEPS, [], [R("epsln")])

        for c in range(2):
            p.dma("pool", i_proj[:, :], dw["proj"][128 * c:128 * (c + 1), :],
                  [], [R("iproj")], tag=f"ld_proj{c}")
            CPY(i_projb[:, :], i_proj[:, :], [R("iproj")], [R("iprojb")])
            TRANS(pi_b[0:DH, 0:128], i_projb[:, :], [R("iprojb")], [R("pib")], tag=f"trp{c}")
            CPY(g_projT2[0:DH, 128 * c:128 * (c + 1)], pi_b[0:DH, 0:128],
                [R("pib")], [R("projT2")])
            ETT(ALU.mult, i_ps[:, :], i_proj[:, :], i_proj[:, :], [R("iproj")], [R("ips")])
            RED("sum", i_col[:, c:c + 1], i_ps[:, :], [R("ips")], [R("icol")])
        CPY(g_projT2[64:128, :], g_projT2[0:64, :], [R("projT2")], [R("projT2")])
        MSET(g_projblk[:, :], 0.0, [], [R("projblk")])
        CPY(g_projblk[0:64, 0:M], g_projT2[0:64, :], [R("projT2"), R("projblk")], [R("projblk")])
        CPY(g_projblk[64:128, M:2 * M], g_projT2[0:64, :], [R("projT2"), R("projblk")], [R("projblk")])
        ETT(ALU.max, i_col[:, 0:1], i_col[:, 0:1], i_col[:, 1:2], [R("icol")], [R("icol")])
        CPY(i_colb[:, :], i_col[:, 0:1], [R("icol")], [R("icolb")])
        TRANS(pi_b[0:1, 0:128], i_colb[:, :], [R("icolb")], [R("pib")], tag="tru")
        RED("max", g_sc[0:1, 0:1], pi_b[0:1, 0:128], [R("pib")], [R("sc0")], tag="umax")
        ACT(g_sc[0:1, 1:2], g_sc[0:1, 0:1], AF.Identity, [R("sc0")], [R("sc1")], scale=0.505)
        MM(pi_t[:, 0:1], g_onesfr[:, :], g_sc[0:1, 1:2], [R("onesfr"), R("sc1")],
           [R("pit")], True, True, tag="bcU")
        ACT(g_negU[:, :], pi_t[:, 0:1], AF.Identity, [R("pit")], [R("negU")], scale=-1.0)
        TS(g_negUle[:, :], g_negU[:, :], math.log(KEPS), None, ALU.add, ALU.bypass,
           [R("negU")], [R("negUle")], tag="negUle")

        bias_list = [("bo", 8), ("cn_b1", 8), ("cn_b2", 4), ("cn_b3", 2),
                     ("gc_b1", 4), ("gc_b2", 2)]
        for bi, (nm, nchunks) in enumerate(bias_list):
            p.dma("pool", i_rows[:, bi, 0:128 * nchunks], dw[nm][0:1, :],
                  [], [R("irow", nm)], tag=f"ld_{nm}")

        def bias_cols_f32(bi, nm, dst, nchunks, dstreg):
            for c in range(nchunks):
                MM(pi_t[:, c:c + 1], i_rows[:, bi, 128 * c:128 * (c + 1)],
                   g_onesf1[:, :],
                   [R("irow", nm), R("onesf1")], [R("pit")], True, True, tag=f"tr_{nm}{c}")
            for c in range(nchunks):
                CPY(dst[:, c:c + 1], pi_t[:, c:c + 1], [R("pit")], [R(dstreg)])

        bias_cols_f32(0, "bo", g_bo, NC8, "g_bo")
        bias_cols_f32(1, "cn_b1", g_b1, NC8, "g_b1")
        bias_cols_f32(2, "cn_b2", g_b2, 4, "g_b2")
        bias_cols_f32(3, "cn_b3", g_b3, 2, "g_b3")
        bias_cols_f32(4, "gc_b1", g_gb1, 4, "g_gb1")
        bias_cols_f32(5, "gc_b2", g_gb2, 2, "g_gb2")

        def wcol_bf(bi, nm, dst, dstreg):
            p.dma("pool", i_rows[:, bi, 0:256], dw[nm][0:1, :],
                  [], [R("irow", nm)], tag=f"ld_{nm}")
            for c in range(2):
                CPY(i_rowb[:, :], i_rows[:, bi, 128 * c:128 * (c + 1)],
                    [R("irow", nm)], [R("irowb")])
                TRANS(pi_b[0:128, 0:1], i_rowb[:, :], [R("irowb")], [R("pib")], tag=f"tw_{nm}{c}", pdim=1)
                CPY(dst[:, c:c + 1], pi_b[0:128, 0:1], [R("pib")], [R(dstreg)])

        wcol_bf(6, "cn_w4r", g_w4, "g_w4")
        wcol_bf(7, "gc_w3r", g_gw3, "g_gw3")

        for c in range(NC8):
            p.dma("pool", g_gw1[:, c, :], dw["gc_w1"][128 * c:128 * (c + 1), :],
                  [], [R("g_gw1", c)], tag=f"ld_gw1{c}")
            p.dma("pool", g_w2m[:, c, :], dw["cn_w2"][128 * c:128 * (c + 1), :],
                  [], [R("g_w2m", c)], tag=f"ld_w2{c}")
        for c in range(4):
            p.dma("pool", g_w3m[:, c, :], dw["cn_w3"][128 * c:128 * (c + 1), :],
                  [], [R("g_w3m", c)], tag=f"ld_w3{c}")
            p.dma("pool", g_gw2[:, c, :], dw["gc_w2"][128 * c:128 * (c + 1), :],
                  [], [R("g_gw2", c)], tag=f"ld_gw2{c}")
        p.dma("pool", g_ar[:, :], dw["arange"], [], [R("ar")], tag="ld_ar")
        p.dma("pool", g_sc[0:1, 2:3], dw["cn_b4"], [], [R("b4r")], tag="ld_b4")
        p.dma("pool", g_sc[0:1, 3:4], dw["gc_b3"], [], [R("gb3r")], tag="ld_gb3")
        ACT(g_halfb4[:, :], g_sc[0:1, 2:3], AF.Identity, [R("b4r")], [R("halfb4")], scale=0.5)
        ACT(g_halfgb3[:, :], g_sc[0:1, 3:4], AF.Identity, [R("gb3r")], [R("halfgb3")], scale=0.5)

    p.barrier()

    def emit_ffn(cid, xin, xout, inpre, outpre):
        tg = f"f{cid}_"
        with ExitStack() as fs:
            FE = fs.enter_context
            w1 = FE(nc.sbuf_tensor(tg + "w1", [128, NC8, D], BF16))
            y = FE(nc.sbuf_tensor(tg + "y", [128, NC8, TOK], F32))
            ybf = FE(nc.sbuf_tensor(tg + "ybf", [128, NT, NC8, TT], BF16))
            xt = FE(nc.sbuf_tensor(tg + "xt", [128, 2, NC8, TT], F32))
            xb = FE(nc.sbuf_tensor(tg + "xb", [128, 2, TT], BF16))
            h1 = FE(nc.sbuf_tensor(tg + "h1", [128, 2, TT], BF16))
            h2 = FE(nc.sbuf_tensor(tg + "h2", [128, 4, TT], BF16))
            h3 = FE(nc.sbuf_tensor(tg + "h3", [128, 3, 2, TT], BF16))
            gsum = FE(nc.sbuf_tensor(tg + "gsum", [128, NC8 * NT], F32))
            gsumr = FE(nc.sbuf_tensor(tg + "gsumr", [128, NC8], F32))
            stat = FE(nc.sbuf_tensor(tg + "stat", [1, 2, 2 * TT], F32))
            gnet = FE(nc.sbuf_tensor(tg + "gnet", [128, 16], BF16))
            scs = FE(nc.sbuf_tensor(tg + "scs", [1, 8], F32))
            tmp8 = FE(nc.sbuf_tensor(tg + "tmp8", [128, 3, TT], F32))
            bc0 = FE(nc.sbuf_tensor(tg + "bc0", [128, 2, TT], F32))
            bc1 = FE(nc.sbuf_tensor(tg + "bc1", [128, 2, TT], F32))
            cbst = FE(nc.sbuf_tensor(tg + "cbst", [128, 2, TT], F32))
            pA = [FE(nc.psum_tensor(tg + f"pA{i}", [128, TT], F32)) for i in range(2)]
            pB = [FE(nc.psum_tensor(tg + f"pB{i}", [128, TT], F32)) for i in range(4)]
            pW = [FE(nc.psum_tensor(tg + f"pW{i}", [128, TT], F32)) for i in range(2)]

            def load_w1():
                for c in range(NC8):
                    p.dma("pool", w1[:, c, :], dw["cn_w1"][128 * c:128 * (c + 1), :],
                          [], [R(tg, "w1", c)], tag=tg + f"w1{c}")


            def L1(t):
                pi = t % 2
                for c in range(NC8):
                    p.dma("sp", xt[:, pi, c, :], xin[128 * c:128 * (c + 1), TT * t:TT * (t + 1)],
                          [R(inpre, c, t)], [R(tg, "xt", pi, c)], tag=tg + f"lx{t}_{c}")
                LN_STATS(tg, lambda c: xt[:, pi, c, :], lambda c: R(tg, "xt", pi, c), xb,
                         pW[0][0:1, 0:TT], R(tg, "pW", 0),
                         pW[1][0:1, 0:TT], R(tg, "pW", 1),
                         stat[:, pi, :], pi,
                         bc0[:, pi, :], R(tg, "bc0", pi),
                         bc1[:, pi, :], R(tg, "bc1", pi), t)

            def BCIN(t):
                pi = t % 2
                s0 = d_bch[16 + 2 * t:17 + 2 * t, 0:TT].squeeze(0).partition_broadcast(128)
                p.dma("sp", bc0[:, pi, :], s0, [R("bch", 16 + 2 * t)],
                      [R(tg, "bc0", pi)], tag=tg + f"bi0{t}")
                s1 = d_bch[17 + 2 * t:18 + 2 * t, 0:TT].squeeze(0).partition_broadcast(128)
                p.dma("sp", bc1[:, pi, :], s1, [R("bch", 17 + 2 * t)],
                      [R(tg, "bc1", pi)], tag=tg + f"bi1{t}")

            def L2(t):
                pi = t % 2
                if cid == 2:
                    for c in range(NC8):
                        p.dma("sp", xt[:, pi, c, :],
                              xin[128 * c:128 * (c + 1), TT * t:TT * (t + 1)],
                              [R(inpre, c, t)], [R(tg, "xt", pi, c)], tag=tg + f"lx{t}_{c}")
                for c in range(NC8):
                    s = c % 3
                    ETT(ALU.mult, tmp8[:, s, :], xt[:, pi, c, :], bc0[:, pi, :],
                        [R(tg, "xt", pi, c), R(tg, "bc0", pi)], [R(tg, "tmp8", s)],
                        eng=("pool" if c % 2 == 0 else "dve"), tag=tg + f"lnm{t}{c}")
                    STT(y[:, c, TT * t:TT * (t + 1)], bc1[:, pi, :], 1.0, tmp8[:, s, :],
                        ALU.bypass, ALU.add,
                        [R(tg, "bc1", pi), R(tg, "tmp8", s)],
                        [R(tg, "y", c, t), R(tg, "gsum", c, t)],
                        accum=gsum[:, NT * c + t:NT * c + t + 1], tag=tg + f"ln{t}_{c}")
                    CPY(ybf[:, t, c, :], y[:, c, TT * t:TT * (t + 1)],
                        [R(tg, "y", c, t)], [R(tg, "ybf", t, c)], tag=tg + f"yc{t}{c}")

            def UW(u):
                cyc, t = divmod(u, NT)
                for m in range(NC8 + 1):
                    if m < NC8:
                        pa, preg = pA[m % 2], R(tg, "pA", m % 2)
                        for k in range(NC8):
                            MM(pa[:, :], w1[:, k, 128 * m:128 * (m + 1)], ybf[:, t, k, :],
                               [R(tg, "w1", k), R(tg, "ybf", t, k)], [preg],
                               k == 0, k == NC8 - 1, tag=tg + f"W1_{u}{m}{k}")
                        ACT(h1[:, m % 2, :], pa[:, :], AF.Gelu, [preg, R("g_b1")],
                            [R(tg, "h1", m % 2)], bias=g_b1[:, m:m + 1],
                            tag=tg + f"H1_{u}{m}")
                    if m >= 1:
                        mm_ = m - 1
                        for m2 in range(4):
                            MM(pB[m2][:, :], g_w2m[:, mm_, 128 * m2:128 * (m2 + 1)],
                               h1[:, mm_ % 2, :],
                               [R("g_w2m", mm_), R(tg, "h1", mm_ % 2)], [R(tg, "pB", m2)],
                               mm_ == 0, mm_ == NC8 - 1, tag=tg + f"W2_{u}{m2}{mm_}")
                            if mm_ == NC8 - 1:
                                ACT(h2[:, m2, :], pB[m2][:, :], AF.Gelu,
                                    [R(tg, "pB", m2), R("g_b2")], [R(tg, "h2", m2)],
                                    bias=g_b2[:, m2:m2 + 1], tag=tg + f"H2_{u}{m2}")

            def T1(u):
                pi3 = u % 3
                for m2 in range(4):
                    for m3 in range(2):
                        MM(pW[m3][:, :], g_w3m[:, m2, 128 * m3:128 * (m3 + 1)], h2[:, m2, :],
                           [R("g_w3m", m2), R(tg, "h2", m2)], [R(tg, "pW", m3)],
                           m2 == 0, m2 == 3, tag=tg + f"W3_{u}{m3}{m2}")
                for m3 in range(2):
                    ACT(h3[:, pi3, m3, :], pW[m3][:, :], AF.Gelu,
                        [R(tg, "pW", m3), R("g_b3")], [R(tg, "h3", pi3, m3)],
                        bias=g_b3[:, m3:m3 + 1], tag=tg + f"H3_{u}{m3}")

            def crowof(u):
                pi2, hf = ((0, 0), (1, 0), (0, 1))[u % 3]
                return (stat[:, pi2, hf * TT:(hf + 1) * TT],
                        R(tg, "st0" if hf == 0 else "st1", pi2))

            def T2a(u):
                pi3 = u % 3
                crow, crreg = crowof(u)
                for m3 in range(2):
                    MM(pB[0][0:1, 0:TT], g_w4[:, m3:m3 + 1], h3[:, pi3, m3, :],
                       [R("g_w4"), R(tg, "h3", pi3, m3)], [R(tg, "pB", 0)],
                       m3 == 0, m3 == 1, tag=tg + f"W4_{u}{m3}")
                ACT(crow, pB[0][0:1, 0:TT], AF.Tanh,
                    [R(tg, "pB", 0), R("halfb4")], [crreg],
                    scale=0.5, bias=g_halfb4[:, :], tag=tg + f"TF_{u}")

            def T2b(u):
                cyc, t = divmod(u, NT)
                pi = u % 2
                last = cyc == 2
                crow, crreg = crowof(u)
                if last:
                    ca_ap, cb_ap = scs[:, 4:5], scs[:, 5:6]
                    ca_reg, cb_reg = R(tg, "ca3"), R(tg, "cb3")
                else:
                    ca_ap, cb_ap = scs[:, 2:3], scs[:, 3:4]
                    ca_reg, cb_reg = R(tg, "ca"), R(tg, "cb")
                TS(crow, crow, cb_ap, ca_ap,
                   ALU.mult, ALU.add,
                   [crreg, ca_reg, cb_reg], [crreg],
                   tag=tg + f"CP_{u}")
                BCAST(crow, [crreg],
                      cbst[:, pi, :], R(tg, "cbst", pi), 128, tag=tg + f"BC_{u}")

            def T3(u):
                cyc, t = divmod(u, NT)
                pi = u % 2
                if cyc < 2:
                    for pr in range(4):
                        cA, cB = pr, pr + 4
                        sA, sB = (2 * pr) % 3, (2 * pr + 1) % 3
                        for c, s in ((cA, sA), (cB, sB)):
                            ETT(ALU.mult, tmp8[:, s, :], cbst[:, pi, :],
                                y[:, TW_PERM[c], TT * t:TT * (t + 1)],
                                [R(tg, "cbst", pi), R(tg, "y", TW_PERM[c], t)],
                                [R(tg, "tmp8", s)], eng="dve", tag=tg + f"tw{u}{c}")
                        for c, s in ((cA, sA), (cB, sB)):
                            sgn = ALU.add if TW_SIGN[c] > 0 else ALU.subtract
                            ya = y[:, c, TT * t:TT * (t + 1)]
                            ETT(sgn, ybf[:, t, c, :], ya, tmp8[:, s, :],
                                [R(tg, "y", c, t), R(tg, "tmp8", s)],
                                [R(tg, "ybf", t, c)], eng="dve", tag=tg + f"yb{u}{c}")
                            ETT(sgn, ya, ya, tmp8[:, s, :],
                                [R(tg, "y", c, t), R(tg, "tmp8", s)], [R(tg, "y", c, t)],
                                eng="pool", tag=tg + f"yu{u}{c}")
                else:
                    pt = t % 2
                    for c in range(NC8):
                        p.dma("sp", xt[:, pt, c, :],
                              xin[128 * c:128 * (c + 1), TT * t:TT * (t + 1)],
                              [R(inpre, c, t)], [R(tg, "xt", pt, c)], tag=tg + f"lq{t}_{c}")
                    for c in range(NC8):
                        ya = y[:, c, TT * t:TT * (t + 1)]
                        ypa = y[:, TW_PERM[c], TT * t:TT * (t + 1)]
                        sw = c % 3
                        sgn = ALU.add if TW_SIGN[c] > 0 else ALU.subtract
                        ETT(ALU.mult, tmp8[:, sw, :], cbst[:, pi, :], ypa,
                            [R(tg, "cbst", pi), R(tg, "y", TW_PERM[c], t)],
                            [R(tg, "tmp8", sw)], eng="dve", tag=tg + f"tw3{t}{c}")
                        STT(xt[:, pt, c, :], ya, 0.5, xt[:, pt, c, :], ALU.mult, ALU.add,
                            [R(tg, "y", c, t), R(tg, "xt", pt, c)], [R(tg, "xt", pt, c)],
                            tag=tg + f"xh{t}{c}")
                        ETT(sgn, xt[:, pt, c, :], xt[:, pt, c, :], tmp8[:, sw, :],
                            [R(tg, "xt", pt, c), R(tg, "tmp8", sw)], [R(tg, "xt", pt, c)],
                            eng="pool", tag=tg + f"xo{t}{c}")
                        p.dma("sp", xout[128 * c:128 * (c + 1), TT * t:TT * (t + 1)],
                              xt[:, pt, c, :], [R(tg, "xt", pt, c)], [R(outpre, c, t)],
                              tag=tg + f"st{t}{c}")

            def gate_launch():
                for c in range(NC8):
                    RED("sum", gsumr[:, c:c + 1], gsum[:, NT * c:NT * (c + 1)],
                        [R(tg, "gsum", c, tt) for tt in range(NT)], [R(tg, "gsumr")],
                        tag=tg + f"gr{c}")
                gi, go = d_gin[cid - 1], d_gout[cid - 1]
                p.dma("sp", gi, gsumr[:, :], [R(tg, "gsumr")], [R(tg, "gi")], tag=tg + "gs")
                p.emit("pool", lambda e, gi=gi, go=go: e.collective_compute(
                    "AllReduce", ALU.add, replica_groups=PAIRS, ins=[gi], outs=[go]),
                    [R(tg, "gi")], [R(tg, "go")], cc=True, tag=tg + "gar")
                p.dma("sp", gsumr[:, :], go, [R(tg, "go")], [R(tg, "gsumr")], tag=tg + "gb")

            def gate_net():
                ACT(gnet[:, 0:NC8], gsumr[:, :], AF.Identity, [R(tg, "gsumr")],
                    [R(tg, "gctx")], scale=1.0 / S)
                for m in range(4):
                    for k in range(NC8):
                        MM(pW[0][:, m:m + 1], g_gw1[:, k, 128 * m:128 * (m + 1)], gnet[:, k:k + 1],
                           [R("g_gw1", k), R(tg, "gctx")], [R(tg, "pW", 0)],
                           k == 0, k == NC8 - 1, tag=tg + f"g1_{m}_{k}")
                    ACT(gnet[:, 8 + m:9 + m], pW[0][:, m:m + 1], AF.Gelu,
                        [R(tg, "pW", 0), R("g_gb1")], [R(tg, "g1", m)],
                        bias=g_gb1[:, m:m + 1], tag=tg + f"g1a{m}")
                for m in range(2):
                    for k in range(4):
                        MM(pW[1][:, m:m + 1], g_gw2[:, k, 128 * m:128 * (m + 1)], gnet[:, 8 + k:9 + k],
                           [R("g_gw2", k), R(tg, "g1", k)], [R(tg, "pW", 1)],
                           k == 0, k == 3, tag=tg + f"g2_{m}_{k}")
                    ACT(gnet[:, 12 + m:13 + m], pW[1][:, m:m + 1], AF.Gelu,
                        [R(tg, "pW", 1), R("g_gb2")], [R(tg, "g2", m)],
                        bias=g_gb2[:, m:m + 1], tag=tg + f"g2a{m}")
                for k in range(2):
                    MM(pB[1][0:1, 0:1], g_gw3[:, k:k + 1], gnet[:, 12 + k:13 + k],
                       [R("g_gw3"), R(tg, "g2", k)], [R(tg, "pB", 1)], k == 0, k == 1,
                       tag=tg + f"g3_{k}")
                ACT(scs[:, 0:1], pB[1][0:1, 0:1], AF.Tanh, [R(tg, "pB", 1), R("halfgb3")],
                    [R(tg, "tgf")], scale=0.5, bias=g_halfgb3[:, :], tag=tg + "gftanh")
                ETT(ALU.mult, scs[:, 1:2], scs[:, 0:1], g_ar[:, :], [R(tg, "tgf"), R("ar")],
                    [R(tg, "t1")])
                TS(scs[:, 2:3], scs[:, 1:2], 0.7, 0.1, ALU.mult, ALU.add,
                   [R(tg, "t1")], [R(tg, "ca")], tag=tg + "ca")
                TS(scs[:, 3:4], g_ar[:, :], 0.3, None, ALU.mult, ALU.bypass,
                   [R("ar")], [R(tg, "cb")], tag=tg + "cb")
                TS(scs[:, 4:5], scs[:, 2:3], 0.5, None, ALU.mult, ALU.bypass,
                   [R(tg, "ca")], [R(tg, "ca3")])
                TS(scs[:, 5:6], scs[:, 3:4], 0.5, None, ALU.mult, ALU.bypass,
                   [R(tg, "cb")], [R(tg, "cb3")])

            load_w1()
            if cid == 1:
                L1(0)
                L1(1)
            else:
                BCIN(0)
                BCIN(1)
            L2(0)
            L2(1)
            UW(0); T1(0)
            L1(2) if cid == 1 else BCIN(2)
            L2(2)
            UW(1); T1(1); T2a(0)
            L1(3) if cid == 1 else BCIN(3)
            L2(3)
            gate_launch()
            UW(2); T1(2); T2a(1)
            UW(3); T1(3); T2a(2)
            gate_net()
            T2b(0); T3(0); T2b(1)
            for u in range(4, 12):
                UW(u); T1(u); T2a(u - 1)
                T3(u - 3); T2b(u - 2)
            T2a(11)
            T3(9); T2b(10)
            T3(10); T2b(11)
            T3(11)

    def emit_attn():
        tg = "a_"
        with ExitStack() as fs:
            FE = fs.enter_context
            qh = FE(nc.sbuf_tensor(tg + "qh", [128, NC8, TOK], BF16))
            ctxs = FE(nc.sbuf_tensor(tg + "ctxs", [65, NC8, TT], F32))
            gTb = FE(nc.sbuf_tensor(tg + "gTb", [16, NT * TT], BF16))
            rmx = FE(nc.sbuf_tensor(tg + "rmx", [128, NST, H], F32))
            ndiag = FE(nc.sbuf_tensor(tg + "ndiag", [128, NST, H], F32))
            edia = FE(nc.sbuf_tensor(tg + "edia", [128, NST, H], F32))
            mrun = FE(nc.sbuf_tensor(tg + "mrun", [128, H], F32))
            mcolb = FE(nc.sbuf_tensor(tg + "mcolb", [128, 1], BF16))
            tmpA = FE(nc.sbuf_tensor(tg + "tmpA", [128, 3, TT], F32))
            bc0 = FE(nc.sbuf_tensor(tg + "bc0", [128, 2, TT], F32))
            bc1 = FE(nc.sbuf_tensor(tg + "bc1", [128, 2, TT], F32))
            xt = FE(nc.sbuf_tensor(tg + "xt", [128, NC8, TT], F32))
            stat = FE(nc.sbuf_tensor(tg + "stat", [1, 2, 2 * TT], F32))
            xb = FE(nc.sbuf_tensor(tg + "xb", [128, 2, TT], BF16))
            gin_ = FE(nc.sbuf_tensor(tg + "gin", [128, 16], F32))
            sq128 = FE(nc.sbuf_tensor(tg + "sq128", [128, 16], BF16))
            vsum = FE(nc.sbuf_tensor(tg + "vsum", [1, D], F32))
            scs = FE(nc.sbuf_tensor(tg + "scs", [1, 8], F32))
            ksq = FE(nc.sbuf_tensor(tg + "ksq", [128, 2, TT], BF16))
            mscr = FE(nc.sbuf_tensor(tg + "mscr", [128, M], BF16))
            ps_a = [FE(nc.psum_tensor(tg + f"psa{i}", [128, TT], F32)) for i in range(2)]
            ps_b = [FE(nc.psum_tensor(tg + f"psb{i}", [65, TT], F32)) for i in range(2)]
            ps_c = FE(nc.psum_tensor(tg + "psc", [128, TT], F32))
            ps_d = [FE(nc.psum_tensor(tg + f"psd{i}", [128, TT], F32)) for i in range(2)]
            ps_e = FE(nc.psum_tensor(tg + "pse", [128, 128], BF16))

            def a_stats(t):
                pi = t % 2
                for c in range(NC8):
                    p.dma("sp", xt[:, c, :], d_x1[128 * c:128 * (c + 1), TT * t:TT * (t + 1)],
                          [R("dx1", c, t)], [R(tg, "xt", c)], tag=tg + f"lx{t}_{c}")
                LN_STATS(tg, lambda c: xt[:, c, :], lambda c: R(tg, "xt", c), xb,
                         ps_d[0][0:1, 0:TT], R(tg, "psd", 0),
                         ps_d[1][0:1, 0:TT], R(tg, "psd", 1),
                         stat[:, pi, :], pi,
                         bc0[:, pi, :], R(tg, "bc0", pi),
                         bc1[:, pi, :], R(tg, "bc1", pi), t)

            def a_bcin(t):
                pi = t % 2
                s0 = d_bch[16 + 2 * t:17 + 2 * t, 0:TT].squeeze(0).partition_broadcast(128)
                p.dma("sp", bc0[:, pi, :], s0, [R("bch", 16 + 2 * t)],
                      [R(tg, "bc0", pi)], tag=tg + f"bi0{t}")
                s1 = d_bch[17 + 2 * t:18 + 2 * t, 0:TT].squeeze(0).partition_broadcast(128)
                p.dma("sp", bc1[:, pi, :], s1, [R("bch", 17 + 2 * t)],
                      [R(tg, "bc1", pi)], tag=tg + f"bi1{t}")

            with ExitStack() as fa:
                FA = fa.enter_context
                wq = FA(nc.sbuf_tensor(tg + "wq", [128, NC8, D], BF16))
                wk = FA(nc.sbuf_tensor(tg + "wk", [128, NC8, D], BF16))
                wv = FA(nc.sbuf_tensor(tg + "wv", [128, NC8, D], BF16))
                ln1 = FA(nc.sbuf_tensor(tg + "ln1", [128, 2, NC8, TT], BF16))
                kb = FA(nc.sbuf_tensor(tg + "kb", [128, NC8, TT], BF16))
                vtm = FA(nc.sbuf_tensor(tg + "vtm", [128, NST, H, 65], BF16))
                ek = FA(nc.sbuf_tensor(tg + "ek", [128, NST, 2 * M], BF16))

                MSET(mrun[:, :], 0.0, [], [R(tg, "mrun")])
                MSET(vsum[:, :], 0.0, [], [R(tg, "vsum")])
                MSET(vtm[:, :, :, :], 1.0, [], [R(tg, "vtm1")], eng="pool")

                def load_wqkv():
                    for c in range(NC8):
                        for wnm, wt in (("wq", wq), ("wk", wk), ("wv", wv)):
                            p.dma("pool", wt[:, c, :], dw[wnm][128 * c:128 * (c + 1), :],
                                  [], [R(tg, wnm, c)], tag=tg + f"{wnm}{c}")

                def a_apply(t):
                    pi = t % 2
                    for c in range(NC8):
                        s = c % 3
                        ETT(ALU.mult, tmpA[:, s, :], xt[:, c, :], bc0[:, pi, :],
                            [R(tg, "xt", c), R(tg, "bc0", pi)], [R(tg, "tmpA", s)],
                            eng=("pool" if c % 2 == 0 else "dve"), tag=tg + f"lnm{t}{c}")
                        ETT(ALU.add, ln1[:, pi, c, :], tmpA[:, s, :], bc1[:, pi, :],
                            [R(tg, "bc1", pi), R(tg, "tmpA", s)], [R(tg, "ln1", pi, c)],
                            eng="dve", tag=tg + f"ln{t}{c}")

                a_stats(0)
                load_wqkv()
                a_apply(0)
                for t in range(NT):
                    pi = t % 2
                    for m in range(NC8):
                        pa, preg = ps_a[m % 2], R(tg, "psa", m % 2)
                        for k in range(NC8):
                            MM(pa[:, :], wq[:, k, 128 * m:128 * (m + 1)], ln1[:, pi, k, :],
                               [R(tg, "wq", k), R(tg, "ln1", pi, k)], [preg],
                               k == 0, k == NC8 - 1, tag=tg + f"Q{t}{m}{k}")
                        ACT(qh[:, m, TT * t:TT * (t + 1)], pa[:, :], AF.Identity, [preg],
                            [R(tg, "qh", m, t)], scale=DN, tag=tg + f"Qe{t}{m}")
                    for m in range(NC8):
                        pa, preg = ps_a[m % 2], R(tg, "psa", m % 2)
                        for k in range(NC8):
                            MM(pa[:, :], wk[:, k, 128 * m:128 * (m + 1)], ln1[:, pi, k, :],
                               [R(tg, "wk", k), R(tg, "ln1", pi, k)], [preg],
                               k == 0, k == NC8 - 1, tag=tg + f"K{t}{m}{k}")
                        ACT(kb[:, m, :], pa[:, :], AF.Identity, [preg],
                            [R(tg, "kb", m)], scale=DN, tag=tg + f"Ke{t}{m}")
                    for st in range(NST):
                        for half in range(2):
                            pa, preg = ps_a[(2 * st + half) % 2], R(tg, "psa", (2 * st + half) % 2)
                            for k in range(NC8):
                                MM(pa[:, :], ln1[:, pi, k, 128 * st:128 * (st + 1)],
                                   wv[:, k, 512 * half:512 * (half + 1)],
                                   [R(tg, "wv", k), R(tg, "ln1", pi, k)], [preg],
                                   k == 0, k == NC8 - 1, tag=tg + f"V{t}{st}{half}{k}")
                            ACT(vtm[:, st, 8 * half:8 * (half + 1), 0:64], pa[:, :],
                                AF.Copy, [preg, R(tg, "vtm1")], [R(tg, "vtm", st, half)],
                                tag=tg + f"Ve{t}{st}{half}")
                    for c in range(NC8):
                        ETT(ALU.mult, ksq[:, c % 2, :], kb[:, c, :], kb[:, c, :],
                            [R(tg, "kb", c)], [R(tg, "ksq", c % 2)], eng="dve")
                        for st in range(NST):
                            MM(ps_c[:, 16 * st + 2 * c:16 * st + 2 * c + 2],
                               ksq[:, c % 2, 128 * st:128 * (st + 1)], g_ind2[:, :],
                               [R(tg, "ksq", c % 2), R("ind2")], [R(tg, "psc")],
                               True, True, tag=tg + f"dg{t}{c}{st}")
                    for st in range(NST):
                        ACT(ndiag[:, st, :], ps_c[:, 16 * st:16 * (st + 1)], AF.Identity,
                            [R(tg, "psc")], [R(tg, "ndiag", st)], scale=-1.0,
                            tag=tg + f"nd{t}{st}")
                        ACT(edia[:, st, :], ps_c[:, 16 * st:16 * (st + 1)], AF.Exp,
                            [R(tg, "psc")], [R(tg, "edia", st)], tag=tg + f"ed{t}{st}")
                    for half in range(2):
                        for st in range(NST):
                            MM(ps_d[half][0:1, :], g_onesbc[:, :],
                               vtm[:, st, 8 * half:8 * (half + 1), 0:64],
                               [R(tg, "vtm", st, half), R("onesbc")], [R(tg, "psd", half)],
                               st == 0, st == NST - 1, tag=tg + f"vs{t}{st}{half}")
                        ETT(ALU.add, vsum[:, 512 * half:512 * (half + 1)],
                            vsum[:, 512 * half:512 * (half + 1)], ps_d[half][0:1, :],
                            [R(tg, "psd", half), R(tg, "vsum")], [R(tg, "vsum")],
                            tag=tg + f"vsa{t}{half}")
                    for hp in range(NC8):
                        pb, pbreg = ps_b[hp % 2], R(tg, "psb", hp % 2)
                        for st in range(NST):
                            pa, preg = ps_a[st % 2], R(tg, "psa", st % 2)
                            MM(pa[:, :], kb[:, hp, 128 * st:128 * (st + 1)], g_projblk[:, :],
                               [R(tg, "kb", hp), R("projblk")], [preg],
                               True, True, tag=tg + f"D{t}{hp}{st}")
                            for half in range(2):
                                h = 2 * hp + half
                                ACT(ek[:, st, M * half:M * (half + 1)], pa[:, M * half:M * (half + 1)],
                                    AF.Exp, [preg, R(tg, "ndiag", st)], [R(tg, "ek", st, half)],
                                    bias=ndiag[:, st, h:h + 1], tag=tg + f"E{t}{hp}{st}{half}")
                                TS(mscr[:, :], ek[:, st, M * half:M * (half + 1)], 1.0, None,
                                   ALU.mult, ALU.max,
                                   [R(tg, "ek", st, half)], [R(tg, "mscr"), R(tg, "rmx", st, h)],
                                   accum=rmx[:, st, h:h + 1], tag=tg + f"RX{t}{hp}{st}{half}")
                        for half in range(2):
                            h = 2 * hp + half
                            for st in range(NST):
                                MM(pb[0:65, M * half:M * (half + 1)],
                                   vtm[:, st, h, 0:65], ek[:, st, M * half:M * (half + 1)],
                                   [R(tg, "vtm", st, h // 8), R(tg, "vtm1"),
                                    R(tg, "ek", st, half)],
                                   [pbreg], st == 0, st == NST - 1, tag=tg + f"C{t}{hp}{st}{half}")
                        if t == 0:
                            CPY(ctxs[:, hp, :], pb[:, :], [pbreg], [R(tg, "ctxs", hp)],
                                tag=tg + f"CA{t}{hp}")
                        else:
                            ETT(ALU.add, ctxs[:, hp, :], ctxs[:, hp, :], pb[:, :],
                                [pbreg, R(tg, "ctxs", hp)], [R(tg, "ctxs", hp)],
                                tag=tg + f"CA{t}{hp}")
                        if hp == 1 and t + 1 < NT:
                            a_stats(t + 1)
                        if hp == 3 and t + 1 < NT:
                            a_apply(t + 1)
                    for st in range(NST):
                        ETT(ALU.mult, rmx[:, st, :], rmx[:, st, :], edia[:, st, :],
                            [R(tg, "edia", st)] + [R(tg, "rmx", st, h) for h in range(H)],
                            [R(tg, "rmx", st, h) for h in range(H)], tag=tg + f"RM{t}{st}")
                        ETT(ALU.max, mrun[:, :], mrun[:, :], rmx[:, st, :],
                            [R(tg, "mrun")] + [R(tg, "rmx", st, h) for h in range(H)],
                            [R(tg, "mrun")], tag=tg + f"MR{t}{st}")

                RED("max", tmpA[:, 0, 0:1], mrun[:, :], [R(tg, "mrun"), R(tg, "tmpA", 0)],
                    [R(tg, "mcol"), R(tg, "tmpA", 0)], tag=tg + "mred")
                CPY(mcolb[:, :], tmpA[:, 0, 0:1], [R(tg, "mcol")], [R(tg, "mcolb")])
                TRANS(ps_e[0:1, 0:128], mcolb[:, :], [R(tg, "mcolb")], [R(tg, "pse")],
                      tag=tg + "mtr")
                RED("max", scs[:, 0:1], ps_e[0:1, 0:128], [R(tg, "pse")], [R(tg, "emax")],
                    tag=tg + "mred2")
                p.dma("sp", d_emin, scs[:, 0:1], [R(tg, "emax")], [R(tg, "emin")],
                      tag=tg + "em_s")
                for c in range(NC8):
                    p.dma("sp", d_ctxin[65 * c:65 * (c + 1), :], ctxs[:, c, :],
                          [R(tg, "ctxs", c)], [R(tg, "ctxin", c)], tag=tg + f"cx_s{c}")
                p.dma("sp", d_vsin, vsum[:, :], [R(tg, "vsum")], [R(tg, "vsin")],
                      tag=tg + "vs_s")
                p.emit("pool", lambda e: e.collective_compute(
                    "AllReduce", ALU.max, replica_groups=ALL8, ins=[d_emin], outs=[d_emout]),
                    [R(tg, "emin")], [R(tg, "emout")], cc=True, tag=tg + "em_ar")
                p.emit("pool", lambda e: e.collective_compute(
                    "AllReduce", ALU.add, replica_groups=PAIRS,
                    ins=[d_ctxin[0:4 * 65, :]], outs=[d_ctxout[0:4 * 65, :]]),
                    [R(tg, "ctxin", c) for c in range(4)],
                    [R(tg, "ctxoutA")], cc=True, tag=tg + "cx_arA")
                p.emit("pool", lambda e: e.collective_compute(
                    "AllReduce", ALU.add, replica_groups=PAIRS,
                    ins=[d_ctxin[4 * 65:8 * 65, :]], outs=[d_ctxout[4 * 65:8 * 65, :]]),
                    [R(tg, "ctxin", c) for c in range(4, NC8)],
                    [R(tg, "ctxoutB")], cc=True, tag=tg + "cx_arB")
                p.emit("pool", lambda e: e.collective_compute(
                    "AllReduce", ALU.add, replica_groups=PAIRS, ins=[d_vsin], outs=[d_vsout]),
                    [R(tg, "vsin")], [R(tg, "vsout")], cc=True, tag=tg + "vs_ar")

                for t in range(NT):
                    for c in range(NC8):
                        ETT(ALU.mult, ksq[:, c % 2, :], qh[:, c, TT * t:TT * (t + 1)],
                            qh[:, c, TT * t:TT * (t + 1)],
                            [R(tg, "qh", c, t)], [R(tg, "ksq", c % 2)], eng="dve")
                        for st in range(NST):
                            MM(ps_c[:, 16 * st + 2 * c:16 * st + 2 * c + 2],
                               ksq[:, c % 2, 128 * st:128 * (st + 1)], g_ind2[:, :],
                               [R(tg, "ksq", c % 2), R("ind2")], [R(tg, "psc")],
                               True, True, tag=tg + f"qdg{t}{c}{st}")
                    for hp in range(NC8):
                        for st in range(NST):
                            pa, preg = ps_a[st % 2], R(tg, "psa", st % 2)
                            MM(pa[:, :], qh[:, hp, TT * t + 128 * st:TT * t + 128 * (st + 1)],
                               g_projblk[:, :], [R(tg, "qh", hp, t), R("projblk")], [preg],
                               True, True, tag=tg + f"qD{t}{hp}{st}")
                            for half in range(2):
                                h = 2 * hp + half
                                TS(mscr[:, :], pa[:, M * half:M * (half + 1)], 1.0, None,
                                   ALU.mult, ALU.max,
                                   [preg], [R(tg, "mscr"), R(tg, "rmx", st, h)],
                                   accum=rmx[:, st, h:h + 1], tag=tg + f"qRX{t}{hp}{st}{half}")
                    for st in range(NST):
                        ETT(ALU.add, gin_[:, :], ps_c[:, 16 * st:16 * (st + 1)], rmx[:, st, :],
                            [R(tg, "psc")] + [R(tg, "rmx", st, h) for h in range(H)],
                            [R(tg, "gin")], tag=tg + f"qg{t}{st}")
                        ACT(gin_[:, :], gin_[:, :], AF.Exp, [R(tg, "gin"), R("negUle")],
                            [R(tg, "gin")], bias=g_negUle[:, :], tag=tg + f"qge{t}{st}")
                        CPY(sq128[:, :], gin_[:, :], [R(tg, "gin")], [R(tg, "sq128")])
                        TRANS(ps_e[0:16, 0:128], sq128[:, :], [R(tg, "sq128")], [R(tg, "pse")],
                              tag=tg + f"qgt{t}{st}")
                        CPY(gTb[:, TT * t + 128 * st:TT * t + 128 * (st + 1)],
                            ps_e[0:16, 0:128], [R(tg, "pse")], [R(tg, "gTb", t)],
                            tag=tg + f"qgc{t}{st}")
            p.barrier()

            with ExitStack() as fb:
                FB = fb.enter_context
                wo = FB(nc.sbuf_tensor(tg + "wo", [128, NC8, D], BF16))
                vse = FB(nc.sbuf_tensor(tg + "vse", [65, H], F32))
                cxb = FB(nc.sbuf_tensor(tg + "cxb", [65, M], BF16))
                ctxT = FB(nc.sbuf_tensor(tg + "ctxT", [128, H * 2 * 65], BF16))
                csM16 = FB(nc.sbuf_tensor(tg + "csM16", [16, H * 65], BF16))
                csmsk = FB(nc.sbuf_tensor(tg + "csmsk", [65, 16], BF16))
                csf = FB(nc.sbuf_tensor(tg + "csf", [65, 1], F32))
                csb = FB(nc.sbuf_tensor(tg + "csb", [65, 1], BF16))
                attn = FB(nc.sbuf_tensor(tg + "attn", [128, 2, NC8, TT], BF16))
                dinvs = FB(nc.sbuf_tensor(tg + "dinvs", [1, 2, TT], F32))
                num = FB(nc.sbuf_tensor(tg + "num", [64, 3, TT], F32))
                dinbs = FB(nc.sbuf_tensor(tg + "dinbs", [64, 3, TT], F32))
                eq = FB(nc.sbuf_tensor(tg + "eq", [128, 2, 2, TT], BF16))
                einv = FB(nc.sbuf_tensor(tg + "einv", [65, 1], F32))
                fsb = FB(nc.sbuf_tensor(tg + "fsb", [128, NC8, TT], BF16))

                for c in range(NC8):
                    p.dma("pool", wo[:, c, :], dw["wo"][128 * c:128 * (c + 1), :],
                          [], [R(tg, "wo", c)], tag=tg + f"wo{c}")

                p.dma("sp", scs[:, 1:2], d_emout, [R(tg, "emout")], [R(tg, "emr")],
                      tag=tg + "em_b")
                p.dve(lambda e: e.reciprocal(scs[:, 2:3], scs[:, 1:2]),
                      [R(tg, "emr")], [R(tg, "einvs")], tag=tg + "einv")
                MM(ps_a[0][0:65, 0:1], g_onesfr[0:1, 0:65], scs[0:1, 2:3],
                   [R(tg, "einvs"), R("onesfr")], [R(tg, "psa", 0)], True, True,
                   tag=tg + "einvbc")
                CPY(einv[:, :], ps_a[0][0:65, 0:1], [R(tg, "psa", 0)], [R(tg, "einvc")])
                p.dma("sp", vsum[:, :], d_vsout, [R(tg, "vsout")], [R(tg, "vsum")],
                      tag=tg + "vs_b")
                for h in range(H):
                    MM(ps_a[0][0:64, 0:1], vsum[0:1, DH * h:DH * (h + 1)], g_onesf1[:, :],
                       [R(tg, "vsum"), R("onesf1")], [R(tg, "psa", 0)], True, True,
                       tag=tg + f"vseT{h}")
                    TS(vse[0:64, h:h + 1], ps_a[0][0:64, 0:1], KEPS, None, ALU.mult,
                       ALU.bypass, [R(tg, "psa", 0)], [R(tg, "vse", h)], tag=tg + f"vse{h}")
                    MSET(vse[64:65, h:h + 1], KEPS * S, [R(tg, "vse", h)], [R(tg, "vse", h)])
                for c in range(NC8):
                    p.dma("sp", ctxs[:, c, :], d_ctxout[65 * c:65 * (c + 1), :],
                          [R(tg, "ctxoutA" if c < 4 else "ctxoutB")], [R(tg, "ctxs", c)],
                          tag=tg + f"cx_b{c}")
                for h in range(H):
                    hp, half = h // 2, h % 2
                    TS(cxb[:, :], ctxs[:, hp, M * half:M * (half + 1)],
                       einv[:, 0:1], vse[:, h:h + 1], ALU.mult, ALU.add,
                       [R(tg, "ctxs", hp), R(tg, "einvc"), R(tg, "vse", h)], [R(tg, "cxb")],
                       tag=tg + f"cfix{h}")
                    RED("sum", csf[:, :], cxb[:, :], [R(tg, "cxb")], [R(tg, "csf")],
                        tag=tg + f"csum{h}")
                    CPY(csb[:, :], csf[:, :], [R(tg, "csf")], [R(tg, "csb")])
                    MSET(csmsk[:, :], 0.0, [R(tg, "csmsk")], [R(tg, "csmsk")])
                    CPY(csmsk[:, h:h + 1], csb[:, :], [R(tg, "csb"), R(tg, "csmsk")],
                        [R(tg, "csmsk")])
                    TRANS(ps_e[0:16, 0:65], csmsk[:, :], [R(tg, "csmsk")], [R(tg, "pse")],
                          tag=tg + f"csT{h}", pdim=65)
                    CPY(csM16[:, 65 * h:65 * (h + 1)], ps_e[0:16, 0:65], [R(tg, "pse")],
                        [R(tg, "csT", h)])
                    for mc in range(2):
                        TRANS(ps_e[0:128, 0:65], cxb[:, 128 * mc:128 * (mc + 1)],
                              [R(tg, "cxb")], [R(tg, "pse")], tag=tg + f"cT{h}{mc}", pdim=65)
                        CPY(ctxT[:, (2 * h + mc) * 65:(2 * h + mc + 1) * 65],
                            ps_e[0:128, 0:65], [R(tg, "pse")], [R(tg, "ctxT", h, mc)])

                def x2M(t, h):
                    hp, half = h // 2, h % 2
                    bse = 64 * half
                    ETT(ALU.mult, attn[bse:bse + 64, t % 2, hp, :], num[:, h % 3, :],
                        dinbs[:, h % 3, :],
                        [R(tg, "num", h % 3), R(tg, "dinb", h % 3)],
                        [R(tg, "attn", t % 2, hp, half)],
                        tag=tg + f"x2M{t}{h}")

                def x2D(t, h):
                    hp, half = h // 2, h % 2
                    banks = ps_a if h % 2 == 0 else ps_d
                    bnm = "psa" if h % 2 == 0 else "psd"
                    for mc in range(2):
                        pa, preg = banks[mc], R(tg, bnm, mc)
                        MM(pa[:, :], g_projT2[64 * half:64 * half + 64, 128 * mc:128 * (mc + 1)],
                           qh[64 * half:64 * half + 64, hp, TT * t:TT * (t + 1)],
                           [R("projT2"), R(tg, "qh", hp, t)], [preg],
                           True, True, tag=tg + f"x2D{t}{h}{mc}")
                        ACT(eq[:, h % 2, mc, :], pa[:, :], AF.Exp,
                            [preg, R("negU")], [R(tg, "eq", h % 2, mc)],
                            bias=g_negU[:, :], tag=tg + f"x2E{t}{h}{mc}")

                def x2AG(t, h):
                    pb, pbreg = ps_b[h % 2], R(tg, "psb", h % 2)
                    for mc in range(2):
                        MM(pb[:, :], ctxT[:, (2 * h + mc) * 65:(2 * h + mc + 1) * 65],
                           eq[:, h % 2, mc, :],
                           [R(tg, "ctxT", h, mc), R(tg, "eq", h % 2, mc)], [pbreg],
                           mc == 0, False, tag=tg + f"x2A{t}{h}{mc}")
                    MM(pb[:, :], csM16[:, 65 * h:65 * (h + 1)],
                       gTb[0:16, TT * t:TT * (t + 1)],
                       [R(tg, "csT", h), R(tg, "gTb", t)], [pbreg],
                       False, True, tag=tg + f"x2G{t}{h}")
                    CPY(num[:, h % 3, :], pb[0:64, :], [pbreg], [R(tg, "num", h % 3)],
                        tag=tg + f"x2N{t}{h}")
                    ACT(dinvs[:, h % 2, :], pb[64:65, :], AF.Ln,
                        [pbreg], [R(tg, "dinv", h % 2)], tag=tg + f"x2R{t}{h}")
                    ACT(dinvs[:, h % 2, :], dinvs[:, h % 2, :], AF.Exp,
                        [R(tg, "dinv", h % 2)], [R(tg, "dinv", h % 2)],
                        scale=-1.0, tag=tg + f"x2R2{t}{h}")
                    BCAST(dinvs[:, h % 2, :], [R(tg, "dinv", h % 2)],
                          dinbs[:, h % 3, :], R(tg, "dinb", h % 3), 64,
                          tag=tg + f"x2B{t}{h}")

                def O_chunk(tt, m):
                    p.dma("sp", xt[:, m, :], d_x1[128 * m:128 * (m + 1), TT * tt:TT * (tt + 1)],
                          [R("dx1", m, tt)], [R(tg, "xt", m)], tag=tg + f"lq{tt}_{m}")
                    for k in range(NC8):
                        MM(ps_c[:, :], wo[:, k, 128 * m:128 * (m + 1)], attn[:, tt % 2, k, :],
                           [R(tg, "wo", k), R(tg, "attn", tt % 2, k, 0), R(tg, "attn", tt % 2, k, 1)],
                           [R(tg, "psc")], k == 0, k == NC8 - 1, tag=tg + f"O{tt}{m}{k}")
                    ACT(tmpA[:, m % 2, :], ps_c[:, :], AF.Identity, [R(tg, "psc"), R("g_bo")],
                        [R(tg, "tmpA", m % 2)], bias=g_bo[:, m:m + 1], tag=tg + f"Oe{tt}{m}")
                    ETT(ALU.add, tmpA[:, m % 2, :], tmpA[:, m % 2, :], xt[:, m, :],
                        [R(tg, "tmpA", m % 2), R(tg, "xt", m)], [R(tg, "tmpA", m % 2)],
                        eng="pool", tag=tg + f"Oa{tt}{m}")
                    p.dma("sp", d_x2[128 * m:128 * (m + 1), TT * tt:TT * (tt + 1)],
                          tmpA[:, m % 2, :], [R(tg, "tmpA", m % 2)], [R("dx2", m, tt)],
                          tag=tg + f"Os{tt}{m}")
                    CPY(fsb[:, m, :], tmpA[:, m % 2, :],
                        [R(tg, "tmpA", m % 2)], [R(tg, "fsb", m)], tag=tg + f"fsc{tt}{m}")

                def O_stats(tt):
                    for m in range(NC8):
                        cst = fsb[:, m, :]
                        MM(ps_d[0][0:1, 0:TT], g_onesbc[:, :], cst,
                           [R(tg, "fsb", m), R("onesbc")], [R(tg, "psd", 0)],
                           m == 0, m == NC8 - 1, tag=tg + f"fsm{tt}{m}")
                        ETT(ALU.mult, cst, cst, cst, [R(tg, "fsb", m)], [R(tg, "fsb", m)])
                        MM(ps_d[1][0:1, 0:TT], g_onesbc[:, :], cst,
                           [R(tg, "fsb", m), R("onesbc")], [R(tg, "psd", 1)],
                           m == 0, m == NC8 - 1, tag=tg + f"fsq{tt}{m}")
                    mean, var = dinvs[:, 0, :], dinvs[:, 1, :]
                    ACT(mean, ps_d[0][0:1, 0:TT], AF.Identity,
                        [R(tg, "psd", 0)], [R(tg, "dinv", 0)], scale=1.0 / D)
                    ETT(ALU.mult, var, mean, mean, [R(tg, "dinv", 0)], [R(tg, "dinv", 1)])
                    STT(var, ps_d[1][0:1, 0:TT], 1.0 / D, var, ALU.mult, ALU.subtract,
                        [R(tg, "psd", 1), R(tg, "dinv", 1)], [R(tg, "dinv", 1)])
                    ACT(var, var, AF.Ln, [R(tg, "dinv", 1), R("epsln")],
                        [R(tg, "dinv", 1)], bias=g_epsln[:, :])
                    ACT(var, var, AF.Exp, [R(tg, "dinv", 1)], [R(tg, "dinv", 1)], scale=-0.5)
                    STT(mean, mean, -1.0, var, ALU.mult, ALU.mult,
                        [R(tg, "dinv", 0), R(tg, "dinv", 1)], [R(tg, "dinv", 0)])
                    p.dma("sp", d_bch[16 + 2 * tt:17 + 2 * tt, :], var,
                          [R(tg, "dinv", 1)], [R("bch", 16 + 2 * tt)], tag=tg + f"fso{tt}")
                    p.dma("sp", d_bch[17 + 2 * tt:18 + 2 * tt, :], mean,
                          [R(tg, "dinv", 0)], [R("bch", 17 + 2 * tt)], tag=tg + f"fsn{tt}")

                for t in range(NT):
                    x2D(t, 0)
                    for h in range(H):
                        if h + 1 < H:
                            x2D(t, h + 1)
                        x2AG(t, h)
                        if h >= 2:
                            x2M(t, h - 2)
                        if t >= 1 and h % 2 == 1:
                            O_chunk(t - 1, h // 2)
                    x2M(t, H - 2)
                    x2M(t, H - 1)
                    if t >= 1:
                        O_stats(t - 1)
                for m in range(NC8):
                    O_chunk(NT - 1, m)
                O_stats(NT - 1)


    emit_ffn(1, d_xT, d_x1, "xTin", "dx1")
    p.barrier()
    emit_attn()
    p.barrier()
    emit_ffn(2, d_x2, d_out, "dx2", "dout")
    counts = p.build()
    gstack.close()
    return nc, counts


_NC_CACHE = None


def _get_nc():
    global _NC_CACHE
    if _NC_CACHE is None:
        _NC_CACHE = build_nc()
    return _NC_CACHE


def kernel(**inputs):
    nc, _ = _get_nc()
    x = np.asarray(inputs["L0_attn_x"], dtype=np.float32)
    row = lambda a: np.ascontiguousarray(np.asarray(a, np.float32).reshape(1, -1))
    base = {
        "wq": np.ascontiguousarray(inputs["L0_attn_wq"], np.float32),
        "wk": np.ascontiguousarray(inputs["L0_attn_wk"], np.float32),
        "wv": np.ascontiguousarray(inputs["L0_attn_wv"], np.float32),
        "wo": np.ascontiguousarray(inputs["L0_attn_wo"], np.float32),
        "proj": np.ascontiguousarray(inputs["L0_attn_proj"], np.float32),
        "cn_w1": np.ascontiguousarray(inputs["L0_ffn_cn_w1"], np.float32),
        "cn_w2": np.ascontiguousarray(inputs["L0_ffn_cn_w2"], np.float32),
        "cn_w3": np.ascontiguousarray(inputs["L0_ffn_cn_w3"], np.float32),
        "gc_w1": np.ascontiguousarray(inputs["L0_ffn_gc_w1"], np.float32),
        "gc_w2": np.ascontiguousarray(inputs["L0_ffn_gc_w2"], np.float32),
        "bo": row(inputs["L0_attn_bo"]),
        "cn_b1": row(inputs["L0_ffn_cn_b1"]),
        "cn_b2": row(inputs["L0_ffn_cn_b2"]),
        "cn_b3": row(inputs["L0_ffn_cn_b3"]),
        "gc_b1": row(inputs["L0_ffn_gc_b1"]),
        "gc_b2": row(inputs["L0_ffn_gc_b2"]),
        "cn_w4r": row(inputs["L0_ffn_cn_w4"]),
        "gc_w3r": row(inputs["L0_ffn_gc_w3"]),
        "cn_b4": row(inputs["L0_ffn_cn_b4"]).reshape(1, 1),
        "gc_b3": row(inputs["L0_ffn_gc_b3"]).reshape(1, 1),
        "arange": row(inputs["L0_ffn_adaptive_range"]).reshape(1, 1),
    }
    in_maps = []
    for core in range(8):
        bidx, half = core // 2, core % 2
        xs = x[bidx, TOK * half:TOK * (half + 1), :]
        m = dict(base)
        m["xT"] = np.ascontiguousarray(xs.T)
        in_maps.append(m)
    res = run_bass_kernel_spmd(nc, in_maps, core_ids=list(range(8)))
    out = np.empty((B, S, D), dtype=np.float32)
    for core in range(8):
        bidx, half = core // 2, core % 2
        out[bidx, TOK * half:TOK * (half + 1), :] = np.asarray(res.results[core]["oT"]).T
    return out


if __name__ == "__main__":
    nc, counts = build_nc()
    try:
        n = nc.n_instructions()
    except Exception:
        n = -1
    print("built;", n, "instructions")
    print("counts:", counts)
